# version 29
# baseline (speedup 1.0000x reference)
import numpy as np
import ml_dtypes
import concourse.bass as bass
import concourse.mybir as mybir
from concourse.bass_utils import run_bass_kernel_spmd
from contextlib import ExitStack

F32 = mybir.dt.float32
BF16 = mybir.dt.bfloat16
I8 = mybir.dt.int8
AF = mybir.ActivationFunctionType
ALU = mybir.AluOpType

NCORES = 8
SEQ = 16384
D = 2048
TC = SEQ // NCORES
H = 16
HD = 128
BLK = 256
NBLK = TC // BLK
INTER = 8192
EPS_RMS = 1e-6
EPS_LN = 1e-5
T2A = 256
N2A = TC // T2A
T2B = 512
N2B = TC // T2B
NEG = -40000.0

ENGINES = ("tensor", "vector", "scalar", "sync", "gpsimd")
MILESTONE = {"tensor": "eT", "vector": "eV", "scalar": "eA"}
SEMS = ("eT", "eV", "eA", "dS", "dC", "dG", "dO", "cc")
DEFSEM = {"tensor": "eT", "vector": "eV", "scalar": "eA"}


class B:

    def __init__(self):
        self.streams = {e: [] for e in ENGINES}
        self.cnt = {n: 0 for n in SEMS}
        self.res = {}

    def w(self, eng, *deps):
        for d in deps:
            if d is not None:
                self.streams[eng].append(("w", d[0], d[1]))

    def o(self, eng, fn, sem=None, amt=1, hard=False):
        self.streams[eng].append(("o", fn, sem, amt, hard))
        if sem is not None:
            self.cnt[sem] += amt
            return (sem, self.cnt[sem])
        return None

    def snap(self):
        return dict(self.cnt)

    def wsnap(self, eng, marks):
        for s, v in marks.items():
            if v > 0:
                self.streams[eng].append(("w", s, v))

    def barrier_all(self, marks=None):
        marks = marks or self.snap()
        for e in ENGINES:
            self.wsnap(e, marks)
        return marks

    DMA_SEMS = ("dS", "dC", "dG", "dO")

    def use(self, eng, reads=(), writes=(), self_sem=None):
        def keep(dep):
            if dep is None:
                return None
            if self_sem is not None and dep[0] == self_sem and self_sem in self.DMA_SEMS:
                return None
            return dep

        for rn in reads:
            e = self.res.get(rn)
            if e and e["w"]:
                self.w(eng, e["w"])
        for rn in writes:
            e = self.res.get(rn)
            if e:
                if e["w"]:
                    self.w(eng, keep(e["w"]))
                for s, v in e["r"].items():
                    self.w(eng, (s, v))

    def did(self, dep, reads=(), writes=()):
        assert dep is not None
        for rn in writes:
            self.res[rn] = {"w": dep, "r": {}}
        for rn in reads:
            e = self.res.setdefault(rn, {"w": None, "r": {}})
            s, v = dep
            if e["r"].get(s, 0) < v:
                e["r"][s] = v

    def E(self, eng, fn, reads=(), writes=(), sem=None, amt=1, hard=False):
        sem = sem or DEFSEM[eng]
        self.use(eng, reads, writes, self_sem=sem)
        dep = self.o(eng, fn, sem, amt, hard)
        self.did(dep, reads, writes)
        return dep

    def G(self, eng, fns, reads=(), writes=(), sem=None):
        sem = sem or DEFSEM[eng]
        self.use(eng, reads, writes, self_sem=sem)
        if sem in self.DMA_SEMS:
            for fn in fns:
                dep = self.o(eng, fn, sem, 16)
        else:
            for i, fn in enumerate(fns):
                last = i == len(fns) - 1
                dep = self.o(eng, fn, sem if last else None, 1, False)
        self.did(dep, reads, writes)
        return dep

    def replay(self, eng_name, eng, semobjs):
        guaranteed = {}
        mysem = MILESTONE.get(eng_name)
        pending_own = 0
        for item in self.streams[eng_name]:
            if item[0] == "w":
                _, s, v = item
                if s == mysem:
                    if v > pending_own:
                        pending_own = v
                    continue
                if guaranteed.get(s, 0) >= v:
                    continue
                eng.wait_ge(semobjs[s], v)
                guaranteed[s] = v
            else:
                _, fn, sem, amt, hard = item
                if (hard or sem in self.DMA_SEMS) and mysem is not None and \
                        pending_own > guaranteed.get(mysem, 0):
                    eng.wait_ge(semobjs[mysem], pending_own)
                    guaranteed[mysem] = pending_own
                inst = fn(eng)
                if sem is not None:
                    inst.then_inc(semobjs[sem], amt)


def build(debug_dump=False):
    nc = bass.Bass("TRN2", target_bir_lowering=False, debug=False)

    x_in = nc.dram_tensor("x", [TC, D], F32, kind="ExternalInput").ap()
    wqkvu = nc.dram_tensor("wqkvu", [4 * D, D], F32, kind="ExternalInput").ap()
    wout = nc.dram_tensor("wout", [D, D], F32, kind="ExternalInput").ap()
    wg_i8 = nc.dram_tensor("wg", [D, INTER], I8, kind="ExternalInput").ap()
    wu_i8 = nc.dram_tensor("wu", [D, INTER], I8, kind="ExternalInput").ap()
    wd_i8 = nc.dram_tensor("wd", [INTER, D], I8, kind="ExternalInput").ap()
    sgate_in = nc.dram_tensor("sgate", [128, 64], F32, kind="ExternalInput").ap()
    sup_in = nc.dram_tensor("sup", [128, 64], F32, kind="ExternalInput").ap()
    sdown_in = nc.dram_tensor("sdown", [1, D], F32, kind="ExternalInput").ap()
    slope_in = nc.dram_tensor("slope", [1, H], F32, kind="ExternalInput").ap()
    mtidx_in = nc.dram_tensor("mtidx", [128, BLK], BF16, kind="ExternalInput").ap()
    arrq_in = nc.dram_tensor("arrq", [128, BLK], BF16, kind="ExternalInput").ap()
    arrkt_in = nc.dram_tensor("arrkt", [128, 2], F32, kind="ExternalInput").ap()
    arrcf_in = nc.dram_tensor("arrcf", [128, NBLK], F32, kind="ExternalInput").ap()
    arrbd_in = nc.dram_tensor("arrbd", [128, 1], F32, kind="ExternalInput").ap()
    iotacc_in = nc.dram_tensor("iotacc", [128, NCORES], F32, kind="ExternalInput").ap()
    idn_in = nc.dram_tensor("idn", [128, 128], BF16, kind="ExternalInput").ap()
    onesbf_in = nc.dram_tensor("onesbf", [128, 1], BF16, kind="ExternalInput").ap()
    onesf32_in = nc.dram_tensor("onesf32", [1, 128], F32, kind="ExternalInput").ap()
    cmisc_in = nc.dram_tensor("cmisc", [128, 6], F32, kind="ExternalInput").ap()

    y_out = nc.dram_tensor("y", [TC, D], F32, kind="ExternalOutput").ap()

    wq_bf = nc.dram_tensor("wq_bf", [4 * D, D], BF16, kind="Internal")
    wo_bf = nc.dram_tensor("wo_bf", [D, D], BF16, kind="Internal")
    wg_bfd = nc.dram_tensor("wg_bfd", [D, INTER], BF16, kind="Internal")
    wu_bfd = nc.dram_tensor("wu_bfd", [D, INTER], BF16, kind="Internal")
    wd_bfd = nc.dram_tensor("wd_bfd", [INTER, D], BF16, kind="Internal")
    x_bf = nc.dram_tensor("x_bf", [TC, D], BF16, kind="Internal")
    o_scr = nc.dram_tensor("o_scr", [H, 128, TC], F32, kind="Internal")
    u_scr = nc.dram_tensor("u_scr", [H, 128, TC], F32, kind="Internal")
    qdec_scr = nc.dram_tensor("qdec_scr", [H, 128, TC], BF16, kind="Internal")
    y1_scr = nc.dram_tensor("y1_scr", [TC, D], F32, kind="Internal")
    y1bf_scr = nc.dram_tensor("y1bf_scr", [TC, D], BF16, kind="Internal")
    cc_in = nc.dram_tensor("cc_in", [H * 128, HD], F32, kind="Internal")
    cc_out_a = nc.dram_tensor("cc_out_a", [NCORES * 8 * 128, HD], F32, kind="Internal")
    cc_out_b = nc.dram_tensor("cc_out_b", [NCORES * 8 * 128, HD], F32, kind="Internal")

    dbg = {}
    if debug_dump:
        for nm, shp, dt in [("dbg_o", [H, 128, TC], F32), ("dbg_u", [H, 128, TC], F32),
                            ("dbg_qdec", [H, 128, TC], F32), ("dbg_y1", [TC, D], F32),
                            ("dbg_kvi", [128, H * HD], F32),
                            ("dbg_y1pre", [TC, D], F32),
                            ("dbg_og", [N2A * 128, H * T2A], F32),
                            ("dbg_st", [N2A * 2, 128, 8], F32),
                            ("dbg_ofull", [N2A * 128, H * T2A], F32),
                            ("dbg_st2", [16, 128, 8], F32)]:
            dbg[nm] = nc.dram_tensor(nm, shp, dt, kind="ExternalOutput")

    b = B()

    es = ExitStack()
    with es:
        semobjs = {s: es.enter_context(nc.semaphore(s)) for s in SEMS}

        c_1bf = es.enter_context(nc.sbuf_tensor("c_1bf", [128, 1], BF16))
        c_1f32 = es.enter_context(nc.sbuf_tensor("c_1f32", [1, 128], F32))
        kvi_bf = es.enter_context(nc.sbuf_tensor("kvi_bf", [128, H * HD], BF16))
        c_cm = es.enter_context(nc.sbuf_tensor("c_cm", [128, 6], F32))

        arena = es.enter_context(nc.psum_tensor("arena", [128, 4096], F32))

        def bank(i, w=512):
            return arena[:, i * 512:i * 512 + w]

        def bank_bf(i):
            return arena[:, i * 512:(i + 1) * 512].bitcast(BF16)

        BANKS = tuple(f"b{i}" for i in range(8))

        b.E("scalar", lambda e: e.dma_start(out=c_cm[:, :], in_=cmisc_in),
            writes=("c_cm",), sem="dC", amt=16)

        def cast_chunk(dst, src, name, i, rows=256):
            b.E("gpsimd", (lambda i: lambda e: e.dma_start(
                out=dst[i * rows:(i + 1) * rows, :],
                in_=src[i * rows:(i + 1) * rows, :]))(i),
                writes=(name,), sem="dG", amt=16)

        for i in range(4):
            cast_chunk(x_bf, x_in, f"xc{i}", i)
        for i in (16, 17):
            cast_chunk(wq_bf, wqkvu, f"qc{i}", i)
        for i in range(4, 8):
            cast_chunk(x_bf, x_in, f"xc{i}", i)
        for i in (18, 19, 20, 21, 22, 23):
            cast_chunk(wq_bf, wqkvu, f"qc{i}", i)
        for i in range(24, 32):
            cast_chunk(wq_bf, wqkvu, f"qc{i}", i)
        for i in range(16):
            cast_chunk(wq_bf, wqkvu, f"qc{i}", i)
        for i in range(8):
            cast_chunk(wo_bf, wout, f"oc{i}", i)
        for i in range(8):
            cast_chunk(wg_bfd, wg_i8, "wgbf", i, rows=256)
        for i in range(8):
            cast_chunk(wu_bfd, wu_i8, "wubf", i, rows=256)
        for i in range(8):
            cast_chunk(wd_bfd, wd_i8, "wdbf", i, rows=1024)
        WOALL = tuple(f"oc{i}" for i in range(8))

        s1 = ExitStack()
        with s1:
            c_mtidx = s1.enter_context(nc.sbuf_tensor("c_mtidx", [128, BLK], BF16))
            c_arrq = s1.enter_context(nc.sbuf_tensor("c_arrq", [128, BLK], BF16))
            c_arrkt = s1.enter_context(nc.sbuf_tensor("c_arrkt", [128, 2], F32))
            c_arrcf = s1.enter_context(nc.sbuf_tensor("c_arrcf", [128, NBLK], F32))
            c_arrbd = s1.enter_context(nc.sbuf_tensor("c_arrbd", [128, 1], F32))
            c_iota = s1.enter_context(nc.sbuf_tensor("c_iota", [128, NCORES], F32))
            c_idn = s1.enter_context(nc.sbuf_tensor("c_idn", [128, 128], BF16))
            c_slope = s1.enter_context(nc.sbuf_tensor("c_slope", [1, H], F32))
            s_bc = s1.enter_context(nc.sbuf_tensor("s_bc", [128, H], F32))
            kvi_f32 = s1.enter_context(nc.sbuf_tensor("kvi_f32", [128, 2 * HD], F32))
            dfh = s1.enter_context(nc.sbuf_tensor("dfh", [128, NCORES], F32))
            for nm, dst, srcp in [("c_mtidx", c_mtidx, mtidx_in), ("c_arrq", c_arrq, arrq_in),
                                  ("c_arrkt", c_arrkt, arrkt_in), ("c_arrcf", c_arrcf, arrcf_in),
                                  ("c_arrbd", c_arrbd, arrbd_in), ("c_iota", c_iota, iotacc_in),
                                  ("c_idn", c_idn, idn_in), ("c_1bf", c_1bf, onesbf_in),
                                  ("c_1f32", c_1f32, onesf32_in), ("c_slope", c_slope, slope_in)]:
                b.E("scalar", (lambda d, s: lambda e: e.dma_start(out=d[:, :], in_=s))(dst, srcp),
                    writes=(nm,), sem="dC", amt=16)
            b.E("tensor", lambda e: nc.tensor.matmul(bank(0, H), c_1f32[0:1, :],
                                                     c_slope[0:1, :], start=True, stop=True),
                reads=("c_1f32", "c_slope"), writes=("b0",))
            b.E("scalar", lambda e: nc.scalar.copy(s_bc[:, :], bank(0, H)),
                reads=("b0",), writes=("s_bc",))

            mt = s1.enter_context(nc.sbuf_tensor("mt", [128, H * BLK], BF16))
            qdt = s1.enter_context(nc.sbuf_tensor("qdt", [128, H * BLK], BF16))
            kdc = s1.enter_context(nc.sbuf_tensor("kdc", [128, H * 2], F32))
            cft = s1.enter_context(nc.sbuf_tensor("cft", [128, H * NBLK], F32))
            bdc = s1.enter_context(nc.sbuf_tensor("bdc", [128, H], F32))
            x_feat = s1.enter_context(nc.sbuf_tensor("x_feat", [128, 16 * TC], BF16))
            v_tok = s1.enter_context(nc.sbuf_tensor("v_tok", [128, 16 * D], BF16))
            wt = s1.enter_context(nc.sbuf_tensor("wt", [128, 16 * 512], BF16))
            wt2 = s1.enter_context(nc.sbuf_tensor("wt2", [128, 16 * 512], BF16))
            wts = [wt, wt2]
            wtn = ["wt", "wt2"]
            q_bf = s1.enter_context(nc.sbuf_tensor("q_bf", [128, TC], BF16))
            k_bf = s1.enter_context(nc.sbuf_tensor("k_bf", [128, TC], BF16))
            qdb = s1.enter_context(nc.sbuf_tensor("qdb", [128, 2 * BLK], BF16))
            kqm = s1.enter_context(nc.sbuf_tensor("kqm", [128, 2 * (BLK + 128)], BF16))
            ktok = s1.enter_context(nc.sbuf_tensor("ktok", [128, TC], BF16))
            vdec = s1.enter_context(nc.sbuf_tensor("vdec", [128, 4 * 128], BF16))
            kv_sb = s1.enter_context(nc.sbuf_tensor("kv_sb", [128, HD], F32))
            kv_bfs = s1.enter_context(nc.sbuf_tensor("kv_bfs", [128, 2 * HD], BF16))
            qdec_sb = s1.enter_context(nc.sbuf_tensor("qdec_sb", [128, 2 * BLK], BF16))
            o_stage = s1.enter_context(nc.sbuf_tensor("o_stage", [128, 2 * BLK], F32))
            u_stage = s1.enter_context(nc.sbuf_tensor("u_stage", [128, 512], F32))

            for h in range(H):
                sb_h = s_bc[:, h:h + 1]
                b.E("scalar", (lambda h, sb_h: lambda e: nc.scalar.activation(
                    mt[:, h * BLK:(h + 1) * BLK], c_mtidx[:, :], AF.Exp, scale=sb_h))(h, sb_h),
                    reads=("c_mtidx", "s_bc"), writes=(f"mt{h}",), hard=True)
                b.E("scalar", (lambda h, sb_h: lambda e: nc.scalar.activation(
                    qdt[:, h * BLK:(h + 1) * BLK], c_arrq[:, :], AF.Exp, scale=sb_h))(h, sb_h),
                    reads=("c_arrq", "s_bc"), writes=(f"qdt{h}",), hard=True)
                b.E("scalar", (lambda h, sb_h: lambda e: nc.scalar.activation(
                    kdc[:, h * 2:(h + 1) * 2], c_arrkt[:, :], AF.Exp, scale=sb_h))(h, sb_h),
                    reads=("c_arrkt", "s_bc"), writes=(f"kdc{h}",), hard=True)
                b.E("scalar", (lambda h, sb_h: lambda e: nc.scalar.activation(
                    cft[:, h * NBLK:(h + 1) * NBLK], c_arrcf[:, :], AF.Exp, scale=sb_h))(h, sb_h),
                    reads=("c_arrcf", "s_bc"), writes=(f"cft{h}",), hard=True)
                b.E("scalar", (lambda h, sb_h: lambda e: nc.scalar.activation(
                    bdc[:, h:h + 1], c_arrbd[:, :], AF.Exp, scale=sb_h))(h, sb_h),
                    reads=("c_arrbd", "s_bc"), writes=(f"bdc{h}",), hard=True)

            def emit_vwt_loads(q):
                wb = wts[q % 2]
                wbn = wtn[q % 2]
                b.G("sync", [
                    (lambda q, ec, wb: lambda e: e.dma_start(
                        out=wb[:, ec * 512:(ec + 1) * 512],
                        in_=wq_bf[2 * D + q * 512:2 * D + (q + 1) * 512,
                                  ec * 128:(ec + 1) * 128], transpose=True))(q, ec, wb)
                    for ec in range(16)],
                    reads=(f"qc{16 + 2 * q}", f"qc{17 + 2 * q}"), writes=(wbn,), sem="dS")

            def emit_xfeat(half):
                b.G("sync", [
                    (lambda ec, half: lambda e: e.dma_start(
                        out=x_feat[:, ec * TC + half * 1024: ec * TC + (half + 1) * 1024],
                        in_=x_bf[half * 1024:(half + 1) * 1024,
                                 ec * 128:(ec + 1) * 128], transpose=True))(ec, half)
                    for ec in range(16)],
                    reads=tuple(f"xc{4 * half + j}" for j in range(4)),
                    writes=(f"x_feat{half}",), sem="dS")

            emit_xfeat(0)
            emit_vwt_loads(0)
            emit_xfeat(1)

            for q in range(4):
                wb = wts[q % 2]
                wbn = wtn[q % 2]
                if q > 0:
                    emit_vwt_loads(q)
                for tc in range(16):
                    bk = tc % 2
                    b.G("tensor", [
                        (lambda tc, ec, bk, wb: lambda e: nc.tensor.matmul(
                            bank(bk),
                            x_feat[:, ec * TC + tc * 128: ec * TC + (tc + 1) * 128],
                            wb[:, ec * 512:(ec + 1) * 512],
                            start=(ec == 0), stop=(ec == 15)))(tc, ec, bk, wb)
                        for ec in range(16)],
                        reads=(f"x_feat{tc // 8}", wbn), writes=(f"b{bk}",))
                    b.E("vector", (lambda q, tc, bk: lambda e: nc.vector.tensor_copy(
                        v_tok[:, tc * D + q * 512: tc * D + (q + 1) * 512], bank(bk)))(q, tc, bk),
                        reads=(f"b{bk}",), writes=("v_tok",))

            for q in range(4):
                wb = wts[q % 2]
                wbn = wtn[q % 2]
                b.G("sync", [
                    (lambda q, ec, wb: lambda e: e.dma_start(
                        out=wb[:, ec * 512:(ec + 1) * 512],
                        in_=wq_bf[3 * D + q * 512:3 * D + (q + 1) * 512,
                                  ec * 128:(ec + 1) * 128], transpose=True))(q, ec, wb)
                    for ec in range(16)],
                    reads=(f"qc{24 + 2 * q}", f"qc{25 + 2 * q}"), writes=(wbn,), sem="dS")
                for fbq in range(4):
                    fb = q * 4 + fbq
                    for ts in range(4):
                        bk = ts % 2
                        b.G("tensor", [
                            (lambda fbq, ts, ec, bk, wb: lambda e: nc.tensor.matmul(
                                bank(bk),
                                wb[:, ec * 512 + fbq * 128: ec * 512 + (fbq + 1) * 128],
                                x_feat[:, ec * TC + ts * 512: ec * TC + (ts + 1) * 512],
                                start=(ec == 0), stop=(ec == 15)))(fbq, ts, ec, bk, wb)
                            for ec in range(16)],
                            reads=(f"x_feat{ts // 2}", wbn), writes=(f"b{bk}",))
                        b.E("scalar", (lambda ts, bk: lambda e: nc.scalar.copy(
                            u_stage[:, :], bank(bk)))(ts, bk),
                            reads=(f"b{bk}",), writes=("u_st",))
                        b.E("scalar", (lambda fb, ts: lambda e: e.dma_start(
                            out=u_scr.ap()[fb, :, ts * 512:(ts + 1) * 512],
                            in_=u_stage[:, :]))(fb, ts),
                            reads=("u_st",), sem="dC", amt=16)

            for g in range(8):
                wb = wts[g % 2]
                wbn = wtn[g % 2]
                b.G("sync", [f for ec in range(16) for f in (
                    (lambda g, ec, wb: lambda e: e.dma_start(
                        out=wb[:, ec * 512:ec * 512 + 256],
                        in_=wq_bf[g * 256:(g + 1) * 256, ec * 128:(ec + 1) * 128],
                        transpose=True))(g, ec, wb),
                    (lambda g, ec, wb: lambda e: e.dma_start(
                        out=wb[:, ec * 512 + 256:(ec + 1) * 512],
                        in_=wq_bf[D + g * 256:D + (g + 1) * 256, ec * 128:(ec + 1) * 128],
                        transpose=True))(g, ec, wb))],
                    reads=(f"qc{g}", f"qc{8 + g}"), writes=(wbn,), sem="dS")

                for hh in range(2):
                    h = 2 * g + hh
                    for which, dst, dstn in (("q", q_bf, "q_bf"), ("k", k_bf, "k_bf")):
                        coloff = hh * 128 if which == "q" else 256 + hh * 128
                        for ts in range(4):
                            bk = ts
                            b.G("tensor", [
                                (lambda coloff, ts, ec, bk, wb: lambda e: nc.tensor.matmul(
                                    bank(bk),
                                    wb[:, ec * 512 + coloff: ec * 512 + coloff + 128],
                                    x_feat[:, ec * TC + ts * 512: ec * TC + (ts + 1) * 512],
                                    start=(ec == 0), stop=(ec == 15)))(coloff, ts, ec, bk, wb)
                                for ec in range(16)],
                                reads=(f"x_feat{ts // 2}", wbn), writes=(f"b{bk}",))
                            b.E("scalar", (lambda dst, ts, bk: lambda e: nc.scalar.activation(
                                dst[:, ts * 512:(ts + 1) * 512], bank(bk), AF.Silu))(dst, ts, bk),
                                reads=(f"b{bk}",), writes=(dstn,))

                    for tsb in range(16):
                        tb = 2 + tsb % 2
                        b.E("tensor", (lambda tsb, tb: lambda e: nc.tensor.transpose(
                            bank_bf(tb)[:, 0:128],
                            k_bf[:, tsb * 128:(tsb + 1) * 128],
                            c_idn[:, :]))(tsb, tb),
                            reads=("k_bf", "c_idn"), writes=(f"b{tb}",))
                        b.E("scalar", (lambda tsb, tb: lambda e: nc.scalar.copy(
                            ktok[:, tsb * 128:(tsb + 1) * 128], bank_bf(tb)[:, 0:128]))(tsb, tb),
                            reads=(f"b{tb}",), writes=("ktok",))

                    def front(h, j):
                        pj = j % 2
                        qo = pj * BLK
                        ko = pj * (BLK + 128)
                        vo = pj * 256
                        b.E("vector", (lambda h, j, qo: lambda e: nc.vector.tensor_tensor(
                            qdb[:, qo:qo + BLK], q_bf[:, j * BLK:(j + 1) * BLK],
                            qdt[:, h * BLK:(h + 1) * BLK], ALU.mult))(h, j, qo),
                            reads=("q_bf", f"qdt{h}"), writes=(f"qdb{pj}",))
                        b.E("vector", (lambda h, j, qo, pj: lambda e: nc.vector.tensor_scalar_mul(
                            qdec_sb[:, pj * BLK:(pj + 1) * BLK], qdb[:, qo:qo + BLK],
                            cft[:, h * NBLK + j: h * NBLK + j + 1]))(h, j, qo, pj),
                            reads=(f"qdb{pj}", f"cft{h}"), writes=(f"qds{pj}",))
                        b.E("scalar", (lambda h, j, pj: lambda e: e.dma_start(
                            out=qdec_scr.ap()[h, :, j * BLK:(j + 1) * BLK],
                            in_=qdec_sb[:, pj * BLK:(pj + 1) * BLK]))(h, j, pj),
                            reads=(f"qds{pj}",), sem="dC", amt=16)
                        for s in (0, 1):
                            bk = 4 + s
                            nq = BLK if s == 0 else 128
                            qoff = 0 if s == 0 else 128
                            b.E("tensor", (lambda h, j, s, bk, nq, qoff: lambda e: nc.tensor.matmul(
                                bank(bk, nq),
                                k_bf[:, j * BLK + s * 128: j * BLK + s * 128 + 128],
                                q_bf[:, j * BLK + qoff: j * BLK + qoff + nq],
                                start=True, stop=True))(h, j, s, bk, nq, qoff),
                                reads=("k_bf", "q_bf"), writes=(f"b{bk}",))
                            b.E("vector", (lambda h, j, s, bk, nq, ko: lambda e: nc.vector.tensor_tensor(
                                kqm[:, ko + s * BLK: ko + s * BLK + nq], bank(bk, nq),
                                mt[:, h * BLK: h * BLK + nq],
                                ALU.mult))(h, j, s, bk, nq, ko),
                                reads=(f"b{bk}", f"mt{h}"), writes=(f"kqm{pj}",))
                            b.E("vector", (lambda h, j, s, vo: lambda e: nc.vector.tensor_scalar_mul(
                                vdec[:, vo + s * 128: vo + (s + 1) * 128],
                                v_tok[:, (2 * j + s) * D + h * 128: (2 * j + s) * D + (h + 1) * 128],
                                kdc[:, h * 2 + s: h * 2 + s + 1]))(h, j, s, vo),
                                reads=("v_tok", f"kdc{h}"), writes=(f"vdec{pj}",))

                    front(h, 0)
                    for j in range(NBLK):
                        pj = j % 2
                        qo = pj * BLK
                        ko = pj * (BLK + 128)
                        vo = pj * 256
                        if j + 1 < NBLK:
                            front(h, j + 1)

                        b.G("tensor", [
                            (lambda j, s, vo: lambda e: nc.tensor.matmul(
                                bank(7, HD), ktok[:, (2 * j + s) * 128:(2 * j + s + 1) * 128],
                                vdec[:, vo + s * 128: vo + (s + 1) * 128],
                                start=(s == 0), stop=(s == 1)))(j, s, vo)
                            for s in (0, 1)],
                            reads=("ktok", f"vdec{pj}"), writes=("b7",))

                        fns = []
                        o_reads = [f"kqm{pj}", "v_tok"]
                        fns.append((lambda h, j, ko: lambda e: nc.tensor.matmul(
                            bank(6, BLK),
                            v_tok[:, (2 * j) * D + h * 128: (2 * j) * D + (h + 1) * 128],
                            kqm[:, ko:ko + BLK],
                            start=True, stop=False))(h, j, ko))
                        fns.append((lambda h, j, ko: lambda e: nc.tensor.matmul(
                            arena[:, 6 * 512 + 128: 6 * 512 + 256],
                            v_tok[:, (2 * j + 1) * D + h * 128: (2 * j + 1) * D + (h + 1) * 128],
                            kqm[:, ko + BLK: ko + BLK + 128],
                            start=False, stop=(j == 0), skip_group_check=True))(h, j, ko))
                        if j > 0:
                            o_reads += [f"kv_bfs{(j - 1) % 2}", f"qdb{pj}"]
                            fns.append((lambda qo, j: lambda e: nc.tensor.matmul(
                                bank(6, BLK),
                                kv_bfs[:, ((j - 1) % 2) * HD:((j - 1) % 2 + 1) * HD],
                                qdb[:, qo:qo + BLK],
                                start=False, stop=True, skip_group_check=True))(qo, j))
                        b.G("tensor", fns, reads=tuple(o_reads), writes=("b6",))

                        par = j % 2
                        b.E("scalar", (lambda par: lambda e: nc.scalar.copy(
                            o_stage[:, par * BLK:(par + 1) * BLK], bank(6, BLK)))(par),
                            reads=("b6",), writes=(f"o_st{par}",))
                        b.E("scalar", (lambda h, j, par: lambda e: e.dma_start(
                            out=o_scr.ap()[h, :, j * BLK:(j + 1) * BLK],
                            in_=o_stage[:, par * BLK:(par + 1) * BLK]))(h, j, par),
                            reads=(f"o_st{par}",), sem="dC", amt=16)

                        if j == 0:
                            b.E("vector", lambda e: nc.vector.tensor_copy(
                                kv_sb[:, :], bank(7, HD)),
                                reads=("b7",), writes=("kv_sb",))
                        else:
                            b.E("vector", (lambda h: lambda e: nc.vector.scalar_tensor_tensor(
                                kv_sb[:, :], kv_sb[:, :], bdc[:, h:h + 1], bank(7, HD),
                                ALU.mult, ALU.add))(h),
                                reads=(f"bdc{h}", "b7", "kv_sb"), writes=("kv_sb",))
                        b.E("scalar", (lambda pj: lambda e: nc.scalar.copy(
                            kv_bfs[:, pj * HD:(pj + 1) * HD], kv_sb[:, :]))(pj),
                            reads=("kv_sb",), writes=(f"kv_bfs{pj}",))

                    b.E("scalar", (lambda h: lambda e: e.dma_start(
                        out=cc_in[h * 128:(h + 1) * 128, :], in_=kv_sb[:, :]))(h),
                        reads=("kv_sb",), writes=(f"cc_in{h}",), sem="dC", amt=16)


                if g == 3:
                    b.E("gpsimd", lambda e: e.collective_compute(
                        "AllGather", ALU.bypass, replica_groups=[list(range(NCORES))],
                        ins=[cc_in.ap()[0:8 * 128, :].opt()], outs=[cc_out_a.ap().opt()]),
                        reads=tuple(f"cc_in{h}" for h in range(8)), writes=("cc_out_a",),
                        sem="cc", amt=1)

            b.E("gpsimd", lambda e: e.collective_compute(
                "AllGather", ALU.bypass, replica_groups=[list(range(NCORES))],
                ins=[cc_in.ap()[8 * 128:H * 128, :].opt()], outs=[cc_out_b.ap().opt()]),
                reads=tuple(f"cc_in{h}" for h in range(8, H)), writes=("cc_out_b",),
                sem="cc", amt=1)

            a_alls = [wt[:, 0:8192].bitcast(F32), wt2[:, 0:8192].bitcast(F32)]
            for hq in range(4):
                a_all = a_alls[hq % 2]
                abn = wtn[hq % 2]
                cc_half = cc_out_a if hq < 2 else cc_out_b
                ccn = "cc_out_a" if hq < 2 else "cc_out_b"
                grp = []
                for hh in range(4):
                    h = hq * 4 + hh
                    src = cc_half.ap().rearrange("(r x) w -> r x w", x=8 * 128)[
                        :, (h % 8) * 128:((h % 8) + 1) * 128, :].rearrange("r p w -> p r w")
                    dst = a_all[:, hh * 1024:(hh + 1) * 1024].rearrange(
                        "p (r w) -> p r w", r=NCORES)
                    grp.append((lambda dst, src: lambda e: e.dma_start(out=dst, in_=src))(dst, src))
                b.G("sync", grp, reads=(ccn,), writes=(abn,), sem="dS")
                for hh in range(4):
                    h = hq * 4 + hh
                    b.E("scalar", (lambda h: lambda e: nc.scalar.activation(
                        dfh[:, :], c_iota[:, :], AF.Exp, scale=s_bc[:, h:h + 1]))(h),
                        reads=("c_iota", "s_bc"), writes=("dfh",), hard=True)
                    b.E("vector", (lambda hh, a_all: lambda e: nc.vector.tensor_scalar_mul(
                        kvi_f32[:, 0:HD], a_all[:, hh * 1024: hh * 1024 + 128],
                        dfh[:, 0:1]))(hh, a_all),
                        reads=(abn, "dfh"), writes=("kvi0",))
                    pp = 0
                    for r in range(1, NCORES):
                        src_col = hh * 1024 + r * 128
                        b.E("vector", (lambda r, pp, src_col, a_all: lambda e: nc.vector.scalar_tensor_tensor(
                            kvi_f32[:, (1 - pp) * HD:(2 - pp) * HD],
                            a_all[:, src_col:src_col + 128],
                            dfh[:, r:r + 1],
                            kvi_f32[:, pp * HD:(pp + 1) * HD],
                            ALU.mult, ALU.add))(r, pp, src_col, a_all),
                            reads=(abn, "dfh", f"kvi{pp}"), writes=(f"kvi{1 - pp}",))
                        pp = 1 - pp
                    b.E("vector", (lambda h, pp: lambda e: nc.vector.tensor_copy(
                        kvi_bf[:, h * HD:(h + 1) * HD],
                        kvi_f32[:, pp * HD:(pp + 1) * HD]))(h, pp),
                        reads=(f"kvi{pp}",), writes=("kvi_bf",))

            if debug_dump:
                b.E("gpsimd", lambda e: e.dma_start(
                    out=dbg["dbg_kvi"].ap()[:, :],
                    in_=kvi_bf[:, :]), reads=("kvi_bf",), sem="dG", amt=16)

        marks_cc = b.barrier_all()

        s2a = ExitStack()
        with s2a:
            woutT = s2a.enter_context(nc.sbuf_tensor("woutT", [128, 16 * D], BF16))
            o_sb = s2a.enter_context(nc.sbuf_tensor("o_sb", [128, 2 * H * T2A], F32))
            u_sb = s2a.enter_context(nc.sbuf_tensor("u_sb", [128, 2 * H * T2A], F32))
            qdec2 = s2a.enter_context(nc.sbuf_tensor("qdec2", [128, 2 * H * T2A], BF16))
            x_sb = s2a.enter_context(nc.sbuf_tensor("x_sb", [128, 2 * D], F32))
            og_sb = s2a.enter_context(nc.sbuf_tensor("og_sb", [128, H * T2A], BF16))
            sq_sb = s2a.enter_context(nc.sbuf_tensor("sq_sb", [128, 3 * D], BF16))
            tmp_f = s2a.enter_context(nc.sbuf_tensor("tmp_f", [128, T2A], F32))
            row_f = s2a.enter_context(nc.sbuf_tensor("row_f", [1, 2 * T2A], F32))
            y1_sb = s2a.enter_context(nc.sbuf_tensor("y1_sb", [128, D], F32))
            y1n_sb = s2a.enter_context(nc.sbuf_tensor("y1n_sb", [128, D], F32))
            y1nbf = s2a.enter_context(nc.sbuf_tensor("y1nbf", [128, D], BF16))
            st_sb = s2a.enter_context(nc.sbuf_tensor("st_sb", [128, 9], F32))

            b.G("scalar", [
                (lambda ec: lambda e: e.dma_start(
                    out=woutT[:, ec * D:(ec + 1) * D],
                    in_=wo_bf[:, ec * 128:(ec + 1) * 128], transpose=True))(ec)
                for ec in range(16)],
                reads=WOALL, writes=("woutT",), sem="dC")

            HT = H * T2A

            def emit_scratch_loads(ti):
                t0 = ti * T2A
                par = ti % 2
                for half in range(2):
                    for scr, dstbuf, dn in ((o_scr, o_sb, "o_sb"), (u_scr, u_sb, "u_sb"),
                                            (qdec_scr, qdec2, "qdec2")):
                        src = scr.ap()[half * 8:(half + 1) * 8, :, t0:t0 + T2A].rearrange(
                            "h p t -> p h t")
                        dst = dstbuf[:, par * HT + half * 8 * T2A:
                                     par * HT + (half + 1) * 8 * T2A].rearrange(
                            "p (h t) -> p h t", h=8)
                        b.E("sync", (lambda dst, src: lambda e: e.dma_start(
                            out=dst, in_=src))(dst, src), writes=(f"{dn}{par}",),
                            sem="dS", amt=16)

            def stageA(ti):
                par = ti % 2
                pb = par * HT
                for h in range(H):
                    cb = 0 if h % 2 == 0 else 3
                    b.E("tensor", (lambda h, pb, cb: lambda e: nc.tensor.matmul(
                        bank(cb, T2A), kvi_bf[:, h * HD:(h + 1) * HD],
                        qdec2[:, pb + h * T2A: pb + (h + 1) * T2A], start=True, stop=True))(h, pb, cb),
                        reads=("kvi_bf", f"qdec2{par}"), writes=(f"b{cb}",))
                    b.E("vector", (lambda h, pb, cb: lambda e: nc.vector.tensor_tensor(
                        o_sb[:, pb + h * T2A: pb + (h + 1) * T2A],
                        o_sb[:, pb + h * T2A: pb + (h + 1) * T2A],
                        bank(cb, T2A), ALU.add))(h, pb, cb),
                        reads=(f"b{cb}", f"o_sb{par}"), writes=(f"o_sb{par}",))

            emit_scratch_loads(0)
            emit_scratch_loads(1)
            stageA(0)
            for ti in range(N2A):
                t0 = ti * T2A
                par = ti % 2
                pb = par * HT
                if ti + 1 < N2A:
                    stageA(ti + 1)
                src = x_in[t0:t0 + T2A, :].rearrange("(tc p) e -> p tc e", p=128)
                dst = x_sb[:, :].rearrange("p (tc e) -> p tc e", tc=2)
                b.E("sync", (lambda dst, src: lambda e: e.dma_start(
                    out=dst, in_=src))(dst, src), writes=("x_sb",), sem="dS", amt=16)

                b.E("vector", (lambda pb: lambda e: nc.vector.tensor_tensor(
                    sq_sb[:, 0:H * T2A], o_sb[:, pb:pb + HT], o_sb[:, pb:pb + HT],
                    ALU.mult))(pb),
                    reads=(f"o_sb{par}",), writes=("sq_sb",))
                b.G("tensor", [
                    (lambda h: lambda e: nc.tensor.matmul(
                        bank(1, T2A)[0:1, :], c_1bf[:, :], sq_sb[:, h * T2A:(h + 1) * T2A],
                        start=(h == 0), stop=(h == 15)))(h)
                    for h in range(H)],
                    reads=("c_1bf", "sq_sb"), writes=("b1",))
                b.E("scalar", lambda e: nc.scalar.activation(
                    row_f[0:1, 0:T2A], bank(1, T2A)[0:1, :], AF.Ln,
                    scale=c_cm[0:1, 0:1], bias=c_cm[0:1, 1:2]),
                    reads=("b1", "c_cm"), writes=("row0",), hard=True)
                b.E("scalar", lambda e: nc.scalar.activation(
                    row_f[0:1, T2A:2 * T2A], row_f[0:1, 0:T2A], AF.Exp,
                    scale=c_cm[0:1, 3:4]),
                    reads=("row0", "c_cm"), writes=("row1",), hard=True)
                b.E("tensor", lambda e: nc.tensor.matmul(
                    bank(2, T2A), c_1f32[0:1, :], row_f[0:1, T2A:2 * T2A],
                    start=True, stop=True),
                    reads=("c_1f32", "row1"), writes=("b2",))

                for h in range(H):
                    b.E("vector", (lambda h, pb: lambda e: nc.vector.tensor_tensor(
                        tmp_f[:, :], o_sb[:, pb + h * T2A: pb + (h + 1) * T2A],
                        bank(2, T2A), ALU.mult))(h, pb),
                        reads=(f"o_sb{par}", "b2"), writes=("tmp_f",))
                    b.E("vector", (lambda h, pb: lambda e: nc.vector.tensor_tensor(
                        og_sb[:, h * T2A:(h + 1) * T2A], tmp_f[:, :],
                        u_sb[:, pb + h * T2A: pb + (h + 1) * T2A], ALU.mult))(h, pb),
                        reads=("tmp_f", f"u_sb{par}"), writes=("og_sb",))

                if ti + 2 < N2A:
                    emit_scratch_loads(ti + 2)

                if debug_dump:
                    b.E("gpsimd", (lambda ti: lambda e: e.dma_start(
                        out=dbg["dbg_og"].ap()[ti * 128:(ti + 1) * 128, :],
                        in_=og_sb[:, :]))(ti),
                        reads=("og_sb",), sem="dG", amt=16)
                    b.E("sync", (lambda ti, pb: lambda e: e.dma_start(
                        out=dbg["dbg_ofull"].ap()[ti * 128:(ti + 1) * 128, :],
                        in_=o_sb[:, pb:pb + HT]))(ti, pb),
                        reads=(f"o_sb{par}",), sem="dS", amt=16)
                for tc2 in range(2):
                    b.G("tensor", [
                        (lambda tc2, ns, h: lambda e: nc.tensor.matmul(
                            arena[:, (4 + ns) * 512:(5 + ns) * 512],
                            og_sb[:, h * T2A + tc2 * 128: h * T2A + tc2 * 128 + 128],
                            woutT[:, h * D + ns * 512: h * D + (ns + 1) * 512],
                            start=(h == 0), stop=(h == 15)))(tc2, ns, h)
                        for ns in range(4) for h in range(H)],
                        reads=("og_sb", "woutT"), writes=("b4", "b5", "b6", "b7"))
                    b.E("vector", (lambda tc2: lambda e: nc.vector.tensor_tensor(
                        y1_sb[:, :], arena[:, 4 * 512: 8 * 512],
                        x_sb[:, tc2 * D:(tc2 + 1) * D], ALU.add))(tc2),
                        reads=("b4", "b5", "b6", "b7", "x_sb"), writes=("y1_sb",))
                    if debug_dump:
                        b.E("sync", (lambda t0, tc2: lambda e: e.dma_start(
                            out=dbg["dbg_y1pre"].ap()[t0 + tc2 * 128: t0 + (tc2 + 1) * 128, :],
                            in_=y1_sb[:, :]))(t0, tc2),
                            reads=("y1_sb",), sem="dS", amt=16)
                    b.E("scalar", lambda e: nc.scalar.activation(
                        sq_sb[:, 0:D], y1_sb[:, :], AF.Copy, accum_out=st_sb[:, 0:1]),
                        reads=("y1_sb",), writes=("sq_sb", "st0"))
                    b.E("vector", lambda e: nc.vector.tensor_tensor(
                        sq_sb[:, D:2 * D], y1_sb[:, :], y1_sb[:, :], ALU.mult),
                        reads=("y1_sb",), writes=("sq_sb",))
                    b.E("scalar", lambda e: nc.scalar.activation(
                        sq_sb[:, 2 * D:3 * D], sq_sb[:, D:2 * D], AF.Copy,
                        accum_out=st_sb[:, 1:2]),
                        reads=("sq_sb",), writes=("sq_sb", "st1"))
                    b.E("scalar", lambda e: nc.scalar.mul(
                        st_sb[:, 2:3], st_sb[:, 0:1], c_cm[:, 0:1]),
                        reads=("st0", "c_cm"), writes=("st2",), hard=True)
                    b.E("scalar", lambda e: nc.scalar.mul(
                        st_sb[:, 3:4], st_sb[:, 1:2], c_cm[:, 0:1]),
                        reads=("st1", "c_cm"), writes=("st3",), hard=True)
                    b.E("scalar", lambda e: nc.scalar.mul(
                        st_sb[:, 4:5], st_sb[:, 2:3], st_sb[:, 2:3]),
                        reads=("st2",), writes=("st4",), hard=True)
                    b.E("scalar", lambda e: nc.scalar.mul(
                        st_sb[:, 8:9], st_sb[:, 0:1], c_cm[:, 0:1]),
                        reads=("st0", "c_cm"), writes=("st8",), hard=True)
                    b.E("scalar", lambda e: nc.scalar.activation(
                        st_sb[:, 5:6], st_sb[:, 4:5], AF.Identity,
                        scale=c_cm[:, 4:5], bias=st_sb[:, 3:4]),
                        reads=("st3", "st4", "c_cm"), writes=("st5",), hard=True)
                    b.E("scalar", lambda e: nc.scalar.activation(
                        st_sb[:, 6:7], st_sb[:, 5:6], AF.Ln, bias=c_cm[:, 2:3]),
                        reads=("st5", "c_cm"), writes=("st6",), hard=True)
                    b.E("scalar", lambda e: nc.scalar.activation(
                        st_sb[:, 6:7], st_sb[:, 6:7], AF.Exp, scale=c_cm[:, 3:4]),
                        reads=("st6", "c_cm"), writes=("st6",), hard=True)
                    b.E("scalar", lambda e: nc.scalar.mul(
                        st_sb[:, 8:9], st_sb[:, 0:1], c_cm[:, 0:1]),
                        reads=("st0", "c_cm"), writes=("st8",), hard=True)
                    b.E("scalar", lambda e: nc.scalar.mul(
                        st_sb[:, 7:8], st_sb[:, 6:7], st_sb[:, 2:3]),
                        reads=("st2", "st6"), writes=("st7",), hard=True)
                    b.E("scalar", lambda e: nc.scalar.mul(
                        st_sb[:, 7:8], st_sb[:, 7:8], c_cm[:, 4:5]),
                        reads=("st7", "c_cm"), writes=("st7",), hard=True)
                    b.E("scalar", lambda e: nc.scalar.mul(
                        st_sb[:, 8:9], st_sb[:, 0:1], c_cm[:, 0:1]),
                        reads=("st0", "c_cm"), writes=("st8",), hard=True)
                    b.E("scalar", lambda e: nc.scalar.activation(
                        y1n_sb[:, :], y1_sb[:, :], AF.Identity,
                        scale=st_sb[:, 6:7], bias=st_sb[:, 7:8]),
                        reads=("y1_sb", "st6", "st7"), writes=("y1n_sb",), hard=True)
                    if debug_dump:
                        b.E("sync", (lambda ti, tc2: lambda e: e.dma_start(
                            out=dbg["dbg_st"].ap()[ti * 2 + tc2, :, :],
                            in_=st_sb[:, 0:8]))(ti, tc2),
                            reads=("st0", "st1", "st2", "st3", "st4", "st5", "st6", "st7"),
                            sem="dS", amt=16)
                    b.E("vector", lambda e: nc.vector.tensor_copy(y1nbf[:, :], y1n_sb[:, :]),
                        reads=("y1n_sb",), writes=("y1nbf",))
                    b.E("scalar", (lambda t0, tc2: lambda e: e.dma_start(
                        out=y1_scr[t0 + tc2 * 128: t0 + (tc2 + 1) * 128, :],
                        in_=y1n_sb[:, :]))(t0, tc2),
                        reads=("y1n_sb",), sem="dC", amt=16)
                    b.E("scalar", (lambda t0, tc2: lambda e: e.dma_start(
                        out=y1bf_scr[t0 + tc2 * 128: t0 + (tc2 + 1) * 128, :],
                        in_=y1nbf[:, :]))(t0, tc2),
                        reads=("y1nbf",), sem="dC", amt=16)

        marks_2a = b.barrier_all()

        if debug_dump:
            for nm, scr in [("dbg_o", o_scr), ("dbg_u", u_scr)]:
                for h in range(H):
                    b.E("sync", (lambda nm, scr, h: lambda e: e.dma_start(
                        out=dbg[nm].ap()[h, :, :], in_=scr.ap()[h, :, :]))(nm, scr, h),
                        sem="dS", amt=16)
            for h in range(H):
                b.E("gpsimd", (lambda h: lambda e: e.dma_start(
                    out=dbg["dbg_qdec"].ap()[h, :, :], in_=qdec_scr.ap()[h, :, :]))(h),
                    sem="dG", amt=16)
            for i in range(4):
                b.E("sync", (lambda i: lambda e: e.dma_start(
                    out=dbg["dbg_y1"].ap()[i * 512:(i + 1) * 512, :],
                    in_=y1_scr[i * 512:(i + 1) * 512, :]))(i),
                    sem="dS", amt=16)

        s2b = ExitStack()
        with s2b:
            y1f = s2b.enter_context(nc.sbuf_tensor("y1f", [128, 16 * T2B], BF16))
            fuse = s2b.enter_context(nc.sbuf_tensor("fuse", [128, 64 * T2B], BF16))
            wgu = s2b.enter_context(nc.sbuf_tensor("wgu", [128, 2 * 16384], BF16))
            wd_sb = s2b.enter_context(nc.sbuf_tensor("wd_sb", [128, 6 * D], BF16))
            sg_sb = s2b.enter_context(nc.sbuf_tensor("sg_sb", [128, 64], F32))
            su_sb = s2b.enter_context(nc.sbuf_tensor("su_sb", [128, 64], F32))
            sdb = s2b.enter_context(nc.sbuf_tensor("sdb", [128, D], F32))
            gs_sb = s2b.enter_context(nc.sbuf_tensor("gs_sb", [128, T2B], BF16))
            y1tok = s2b.enter_context(nc.sbuf_tensor("y1tok", [128, D], F32))
            y2n_sb = s2b.enter_context(nc.sbuf_tensor("y2n_sb", [128, D], F32))
            st2 = s2b.enter_context(nc.sbuf_tensor("st2", [128, 9], F32))
            dum2 = s2b.enter_context(nc.sbuf_tensor("dum2", [128, 2 * D], BF16))

            b.E("sync", lambda e: e.dma_start(out=sg_sb[:, :], in_=sgate_in),
                writes=("sg_sb",), sem="dS", amt=16)
            b.E("sync", lambda e: e.dma_start(out=su_sb[:, :], in_=sup_in),
                writes=("su_sb",), sem="dS", amt=16)
            b.E("sync", lambda e: e.dma_start(out=y1tok[0:1, 0:D], in_=sdown_in),
                writes=("sdr",), sem="dS", amt=16)
            b.G("tensor", [
                (lambda ns: lambda e: nc.tensor.matmul(
                    arena[:, (4 + ns) * 512:(5 + ns) * 512], c_1f32[0:1, :],
                    y1tok[0:1, ns * 512:(ns + 1) * 512], start=True, stop=True))(ns)
                for ns in range(4)],
                reads=("c_1f32", "sdr"), writes=("b4", "b5", "b6", "b7"))
            b.E("scalar", lambda e: nc.scalar.copy(sdb[:, :], arena[:, 2048:4096]),
                reads=("b4", "b5", "b6", "b7"), writes=("sdb",))

            def emit_y1f_loads(ti):
                t0 = ti * T2B
                b.G("sync", [
                    (lambda t0, hc: lambda e: e.dma_start(
                        out=y1f[:, hc * T2B:(hc + 1) * T2B],
                        in_=y1bf_scr[t0:t0 + T2B, hc * 128:(hc + 1) * 128],
                        transpose=True))(t0, hc)
                    for hc in range(16)],
                    writes=("y1f",), sem="dS")

            def emit_wgu_loads(ti, ig):
                pb = ig % 2
                base = pb * 16384
                for half in range(2):
                    eng, sem = ("sync", "dS") if half == 0 else ("scalar", "dC")
                    grp = []
                    for mat, off in ((wg_bfd, base), (wu_bfd, base + 8192)):
                        srcp = mat[half * 1024:(half + 1) * 1024,
                                   ig * 512:(ig + 1) * 512].rearrange(
                            "(hc p) i -> p hc i", p=128)
                        dst = wgu[:, off + half * 8 * 512: off + (half + 1) * 8 * 512].rearrange(
                            "p (hc i) -> p hc i", hc=8)
                        grp.append((lambda dst, srcp: lambda e: e.dma_start(
                            out=dst, in_=srcp))(dst, srcp))
                    b.G(eng, grp, reads=("wgbf", "wubf"),
                        writes=(f"wgu{pb}h{half}",), sem=sem)

            loaded = set()
            emit_y1f_loads(0)
            for ti in range(N2B):
                t0 = ti * T2B

                for ig in range(16):
                    pb = ig % 2
                    base = pb * 16384
                    if (ti, ig) not in loaded:
                        emit_wgu_loads(ti, ig)
                        loaded.add((ti, ig))

                    for ic4 in range(4):
                        b.G("tensor", [
                            (lambda base, ic4, hc: lambda e: nc.tensor.matmul(
                                bank(ic4),
                                wgu[:, base + hc * 512 + ic4 * 128: base + hc * 512 + ic4 * 128 + 128],
                                y1f[:, hc * T2B:(hc + 1) * T2B],
                                start=(hc == 0), stop=(hc == 15)))(base, ic4, hc)
                            for hc in range(16)],
                            reads=(f"wgu{pb}h0", f"wgu{pb}h1", "y1f"), writes=(f"b{ic4}",))
                        b.G("tensor", [
                            (lambda base, ic4, hc: lambda e: nc.tensor.matmul(
                                bank(4 + ic4),
                                wgu[:, base + 8192 + hc * 512 + ic4 * 128: base + 8192 + hc * 512 + ic4 * 128 + 128],
                                y1f[:, hc * T2B:(hc + 1) * T2B],
                                start=(hc == 0), stop=(hc == 15)))(base, ic4, hc)
                            for hc in range(16)],
                            reads=(f"wgu{pb}h0", f"wgu{pb}h1", "y1f"), writes=(f"b{4 + ic4}",))
                        ic = ig * 4 + ic4
                        b.E("scalar", (lambda ic, ic4: lambda e: nc.scalar.activation(
                            gs_sb[:, :], bank(ic4), AF.Silu,
                            scale=sg_sb[:, ic:ic + 1]))(ic, ic4),
                            reads=(f"b{ic4}", "sg_sb"), writes=("gs_sb",))
                        b.E("vector", (lambda ic, ic4: lambda e: nc.vector.scalar_tensor_tensor(
                            fuse[:, ic * T2B:(ic + 1) * T2B], bank(4 + ic4),
                            su_sb[:, ic:ic + 1], gs_sb[:, :],
                            ALU.mult, ALU.mult))(ic, ic4),
                            reads=(f"b{4 + ic4}", "su_sb", "gs_sb"), writes=("fuse",))

                if ti + 1 < N2B:
                    emit_wgu_loads(ti + 1, 0)
                    emit_wgu_loads(ti + 1, 1)
                    loaded.add((ti + 1, 0))
                    loaded.add((ti + 1, 1))
                    emit_y1f_loads(ti + 1)

                for tp in range(2):
                    b.E("sync", (lambda t0, tp: lambda e: e.dma_start(
                        out=y1tok[:, :],
                        in_=y1_scr[t0 + tp * 256: t0 + tp * 256 + 128, :]))(t0, tp),
                        writes=("y1tok", "sdr"), sem="dS", amt=16)

                    for ip in range(32):
                        par = ip % 3
                        src2 = wd_bfd[ip * 256:(ip + 1) * 256, :].rearrange(
                            "(a p) h -> p a h", p=128)
                        dst2 = wd_sb[:, par * 2 * D:(par + 1) * 2 * D].rearrange(
                            "p (a h) -> p a h", a=2)
                        eng, sem = ("sync", "dS") if ip % 2 == 0 else ("scalar", "dC")
                        b.E(eng, (lambda dst2, src2: lambda e: e.dma_start(
                            out=dst2, in_=src2))(dst2, src2),
                            reads=("wdbf",), writes=(f"wd{par}",), sem=sem, amt=16)
                        fns = []
                        for a in range(2):
                            i = ip * 2 + a
                            for tc2 in range(2):
                                tg = tp * 2 + tc2
                                for ns in range(4):
                                    fns.append((lambda i, par, a, tc2, tg, ns: lambda e: nc.tensor.matmul(
                                        arena[:, (tc2 * 4 + ns) * 512:(tc2 * 4 + ns + 1) * 512],
                                        fuse[:, i * T2B + tg * 128: i * T2B + (tg + 1) * 128],
                                        wd_sb[:, par * 2 * D + a * D + ns * 512:
                                               par * 2 * D + a * D + (ns + 1) * 512],
                                        start=(i == 0), stop=(i == 63),
                                        skip_group_check=True))(i, par, a, tc2, tg, ns))
                        b.G("tensor", fns, reads=("fuse", f"wd{par}"), writes=BANKS)

                    for tc2 in range(2):
                        tg = tp * 2 + tc2
                        bs = ("b0", "b1", "b2", "b3") if tc2 == 0 else ("b4", "b5", "b6", "b7")
                        b.E("vector", (lambda tc2: lambda e: nc.vector.tensor_tensor(
                            y2n_sb[:, :], arena[:, tc2 * 2048: (tc2 + 1) * 2048],
                            sdb[:, :], ALU.mult))(tc2),
                            reads=bs + ("sdb",), writes=("y2buf",))
                        b.E("vector", lambda e: nc.vector.tensor_tensor(
                            y2n_sb[:, :], y2n_sb[:, :], y1tok[:, :],
                            ALU.add), reads=("y1tok", "y2buf"), writes=("y2buf",))
                        if tc2 == 0:
                            b.E("sync", (lambda t0, tp: lambda e: e.dma_start(
                                out=y1tok[:, :],
                                in_=y1_scr[t0 + tp * 256 + 128: t0 + tp * 256 + 256, :]))(t0, tp),
                                writes=("y1tok",), sem="dS", amt=16)
                        b.E("scalar", lambda e: nc.scalar.activation(
                            dum2[:, 0:D], y2n_sb[:, :], AF.Copy, accum_out=st2[:, 0:1]),
                            reads=("y2buf",), writes=("dum2", "s20"))
                        b.E("vector", lambda e: nc.vector.tensor_tensor(
                            dum2[:, D:2 * D], y2n_sb[:, :], y2n_sb[:, :], ALU.mult),
                            reads=("y2buf",), writes=("dum2",))
                        b.E("scalar", lambda e: nc.scalar.activation(
                            dum2[:, 0:D], dum2[:, D:2 * D], AF.Copy,
                            accum_out=st2[:, 1:2]),
                            reads=("dum2",), writes=("dum2", "s21"))
                        b.E("scalar", lambda e: nc.scalar.mul(
                            st2[:, 2:3], st2[:, 0:1], c_cm[:, 0:1]),
                            reads=("s20", "c_cm"), writes=("s22",), hard=True)
                        b.E("scalar", lambda e: nc.scalar.mul(
                            st2[:, 3:4], st2[:, 1:2], c_cm[:, 0:1]),
                            reads=("s21", "c_cm"), writes=("s23",), hard=True)
                        b.E("scalar", lambda e: nc.scalar.mul(
                            st2[:, 4:5], st2[:, 2:3], st2[:, 2:3]),
                            reads=("s22",), writes=("s24",), hard=True)
                        b.E("scalar", lambda e: nc.scalar.mul(
                            st2[:, 8:9], st2[:, 0:1], c_cm[:, 0:1]),
                            reads=("s20", "c_cm"), writes=("s28",), hard=True)
                        b.E("scalar", lambda e: nc.scalar.activation(
                            st2[:, 5:6], st2[:, 4:5], AF.Identity,
                            scale=c_cm[:, 4:5], bias=st2[:, 3:4]),
                            reads=("s23", "s24", "c_cm"), writes=("s25",), hard=True)
                        b.E("scalar", lambda e: nc.scalar.activation(
                            st2[:, 6:7], st2[:, 5:6], AF.Ln, bias=c_cm[:, 2:3]),
                            reads=("s25", "c_cm"), writes=("s26",), hard=True)
                        b.E("scalar", lambda e: nc.scalar.activation(
                            st2[:, 6:7], st2[:, 6:7], AF.Exp, scale=c_cm[:, 3:4]),
                            reads=("s26", "c_cm"), writes=("s26",), hard=True)
                        b.E("scalar", lambda e: nc.scalar.mul(
                            st2[:, 8:9], st2[:, 0:1], c_cm[:, 0:1]),
                            reads=("s20", "c_cm"), writes=("s28",), hard=True)
                        b.E("scalar", lambda e: nc.scalar.mul(
                            st2[:, 7:8], st2[:, 6:7], st2[:, 2:3]),
                            reads=("s22", "s26"), writes=("s27",), hard=True)
                        b.E("scalar", lambda e: nc.scalar.mul(
                            st2[:, 7:8], st2[:, 7:8], c_cm[:, 4:5]),
                            reads=("s27", "c_cm"), writes=("s27",), hard=True)
                        b.E("scalar", lambda e: nc.scalar.mul(
                            st2[:, 8:9], st2[:, 0:1], c_cm[:, 0:1]),
                            reads=("s20", "c_cm"), writes=("s28",), hard=True)
                        b.E("scalar", (lambda tc2: lambda e: nc.scalar.activation(
                            y1tok[:, :], y2n_sb[:, :], AF.Identity,
                            scale=st2[:, 6:7], bias=st2[:, 7:8]))(tc2),
                            reads=("y2buf", "s26", "s27"), writes=("y1tok", "y2n"), hard=True)
                        if debug_dump:
                            idx = ti * 4 + tp * 2 + tc2
                            b.E("sync", (lambda idx: lambda e: e.dma_start(
                                out=dbg["dbg_st2"].ap()[idx, :, :], in_=st2[:, 0:8]))(idx),
                                reads=("s20", "s21", "s22", "s23", "s24", "s25", "s26", "s27"),
                                sem="dS", amt=16)
                        b.E("scalar", (lambda t0, tg: lambda e: e.dma_start(
                            out=y_out[t0 + tg * 128: t0 + (tg + 1) * 128, :],
                            in_=y1tok[:, :]))(t0, tg),
                            reads=("y2n", "y1tok"), sem="dO", amt=16)

        b.w("sync", ("dO", b.cnt["dO"]))
        b.o("sync", lambda e: e.nop())
        b.w("scalar", ("dO", b.cnt["dO"]))
        b.o("scalar", lambda e: e.nop())

        with nc.Block() as block:
            @block.tensor
            def _(eng):
                b.replay("tensor", eng, semobjs)

            @block.vector
            def _(eng):
                b.replay("vector", eng, semobjs)

            @block.scalar
            def _(eng):
                b.replay("scalar", eng, semobjs)

            @block.sync
            def _(eng):
                b.replay("sync", eng, semobjs)

            @block.gpsimd
            def _(eng):
                b.replay("gpsimd", eng, semobjs)

    return nc


def _host_consts(core):
    p = np.arange(128)
    m = np.arange(BLK)
    mtidx = np.zeros((128, BLK), np.float32)
    d = m[None, :] - p[:, None]
    mtidx[:, :] = np.where(d >= 0, -d, NEG)
    mtidx = mtidx.astype(ml_dtypes.bfloat16)
    arrq = np.ascontiguousarray(
        np.broadcast_to(-(m[None, :] + 1.0), (128, BLK))).astype(ml_dtypes.bfloat16)
    arrkt = np.zeros((128, 2), np.float32)
    for half in (0, 1):
        arrkt[:, half] = -(255.0 - (half * 128 + p))
    arrcf = np.ascontiguousarray(
        np.broadcast_to(-256.0 * np.arange(NBLK)[None, :], (128, NBLK))).astype(np.float32)
    arrbd = np.full((128, 1), -256.0, np.float32)
    iot = np.zeros((128, NCORES), np.float32)
    for r in range(NCORES):
        iot[:, r] = (-2048.0 * (core - 1 - r)) if r < core else NEG
    idn = np.eye(128, dtype=ml_dtypes.bfloat16)
    onesbf = np.ones((128, 1), ml_dtypes.bfloat16)
    onesf32 = np.ones((1, 128), np.float32)
    cmisc = np.zeros((128, 6), np.float32)
    cmisc[:, 0] = 1.0 / D
    cmisc[:, 1] = EPS_RMS
    cmisc[:, 2] = EPS_LN
    cmisc[:, 3] = -0.5
    cmisc[:, 4] = -1.0
    return dict(mtidx=mtidx, arrq=arrq, arrkt=arrkt, arrcf=arrcf, arrbd=arrbd,
                iotacc=iot, idn=idn, onesbf=onesbf, onesf32=onesf32, cmisc=cmisc)


_NC_CACHE = {}


def _get_nc(debug_dump=False):
    if debug_dump not in _NC_CACHE:
        _NC_CACHE[debug_dump] = build(debug_dump)
    return _NC_CACHE[debug_dump]


def kernel(L0_attn_x, L0_attn_slope_rate, L0_attn_W_qkvu, L0_attn_W_out,
           L0_ffn_w_gate, L0_ffn_w_up, L0_ffn_w_down,
           L0_ffn_s_gate, L0_ffn_s_up, L0_ffn_s_down, _debug=False):
    x = np.ascontiguousarray(np.asarray(L0_attn_x, np.float32).reshape(SEQ, D))
    wq = np.ascontiguousarray(np.asarray(L0_attn_W_qkvu, np.float32))
    wo = np.ascontiguousarray(np.asarray(L0_attn_W_out, np.float32))
    wg = np.ascontiguousarray(np.asarray(L0_ffn_w_gate, np.int8))
    wu = np.ascontiguousarray(np.asarray(L0_ffn_w_up, np.int8))
    wd = np.ascontiguousarray(np.asarray(L0_ffn_w_down, np.int8))
    sg = np.ascontiguousarray(np.asarray(L0_ffn_s_gate, np.float32).reshape(64, 128).T)
    su = np.ascontiguousarray(np.asarray(L0_ffn_s_up, np.float32).reshape(64, 128).T)
    sd = np.ascontiguousarray(np.asarray(L0_ffn_s_down, np.float32).reshape(1, D))
    sl = np.ascontiguousarray(np.asarray(L0_attn_slope_rate, np.float32).reshape(1, H))

    nc = _get_nc(_debug)
    in_maps = []
    for c in range(NCORES):
        m = dict(x=np.ascontiguousarray(x[c * TC:(c + 1) * TC]),
                 wqkvu=wq, wout=wo, wg=wg, wu=wu, wd=wd,
                 sgate=sg, sup=su, sdown=sd, slope=sl)
        m.update(_host_consts(c))
        in_maps.append(m)
    res = run_bass_kernel_spmd(nc, in_maps, core_ids=list(range(NCORES)))
    if _debug:
        return res
    out = np.concatenate([np.asarray(res.results[c]["y"]) for c in range(NCORES)], axis=0)
    return out.reshape(1, SEQ, D)


# version 32
# speedup vs baseline: 1.1756x; 1.1756x over previous
import numpy as np
import ml_dtypes
import concourse.bass as bass
import concourse.mybir as mybir
from concourse.bass_utils import run_bass_kernel_spmd
from contextlib import ExitStack

F32 = mybir.dt.float32
BF16 = mybir.dt.bfloat16
I8 = mybir.dt.int8
AF = mybir.ActivationFunctionType
ALU = mybir.AluOpType

NCORES = 8
SEQ = 16384
D = 2048
TC = SEQ // NCORES
H = 16
HD = 128
BLK = 256
NBLK = TC // BLK
INTER = 8192
EPS_RMS = 1e-6
EPS_LN = 1e-5
T2A = 256
N2A = TC // T2A
T2B = 512
N2B = TC // T2B
NEG = -40000.0

ENGINES = ("tensor", "vector", "scalar", "sync", "gpsimd")
MILESTONE = {"tensor": "eT", "vector": "eV", "scalar": "eA"}
SEMS = ("eT", "eV", "eA", "dS", "dC", "dG", "dO", "cc")
DEFSEM = {"tensor": "eT", "vector": "eV", "scalar": "eA"}


class B:

    def __init__(self):
        self.streams = {e: [] for e in ENGINES}
        self.cnt = {n: 0 for n in SEMS}
        self.res = {}

    def w(self, eng, *deps):
        for d in deps:
            if d is not None:
                self.streams[eng].append(("w", d[0], d[1]))

    def o(self, eng, fn, sem=None, amt=1, hard=False):
        self.streams[eng].append(("o", fn, sem, amt, hard))
        if sem is not None:
            self.cnt[sem] += amt
            return (sem, self.cnt[sem])
        return None

    def snap(self):
        return dict(self.cnt)

    def wsnap(self, eng, marks):
        for s, v in marks.items():
            if v > 0:
                self.streams[eng].append(("w", s, v))

    def barrier_all(self, marks=None):
        marks = marks or self.snap()
        for e in ENGINES:
            self.wsnap(e, marks)
        return marks

    DMA_SEMS = ("dS", "dC", "dG", "dO")

    def use(self, eng, reads=(), writes=(), self_sem=None):
        def keep(dep):
            if dep is None:
                return None
            if self_sem is not None and dep[0] == self_sem and self_sem in self.DMA_SEMS:
                return None
            return dep

        for rn in reads:
            e = self.res.get(rn)
            if e and e["w"]:
                self.w(eng, e["w"])
        for rn in writes:
            e = self.res.get(rn)
            if e:
                if e["w"]:
                    self.w(eng, keep(e["w"]))
                for s, v in e["r"].items():
                    self.w(eng, (s, v))

    def did(self, dep, reads=(), writes=()):
        assert dep is not None
        for rn in writes:
            self.res[rn] = {"w": dep, "r": {}}
        for rn in reads:
            e = self.res.setdefault(rn, {"w": None, "r": {}})
            s, v = dep
            if e["r"].get(s, 0) < v:
                e["r"][s] = v

    def E(self, eng, fn, reads=(), writes=(), sem=None, amt=1, hard=False):
        sem = sem or DEFSEM[eng]
        self.use(eng, reads, writes, self_sem=sem)
        dep = self.o(eng, fn, sem, amt, hard)
        self.did(dep, reads, writes)
        return dep

    def G(self, eng, fns, reads=(), writes=(), sem=None):
        sem = sem or DEFSEM[eng]
        self.use(eng, reads, writes, self_sem=sem)
        if sem in self.DMA_SEMS:
            for fn in fns:
                dep = self.o(eng, fn, sem, 16)
        else:
            for i, fn in enumerate(fns):
                last = i == len(fns) - 1
                dep = self.o(eng, fn, sem if last else None, 1, False)
        self.did(dep, reads, writes)
        return dep

    def replay(self, eng_name, eng, semobjs):
        guaranteed = {}
        mysem = MILESTONE.get(eng_name)
        pending_own = 0
        for item in self.streams[eng_name]:
            if item[0] == "w":
                _, s, v = item
                if s == mysem:
                    if v > pending_own:
                        pending_own = v
                    continue
                if guaranteed.get(s, 0) >= v:
                    continue
                eng.wait_ge(semobjs[s], v)
                guaranteed[s] = v
            else:
                _, fn, sem, amt, hard = item
                if (hard or sem in self.DMA_SEMS) and mysem is not None and \
                        pending_own > guaranteed.get(mysem, 0):
                    eng.wait_ge(semobjs[mysem], pending_own)
                    guaranteed[mysem] = pending_own
                inst = fn(eng)
                if sem is not None:
                    inst.then_inc(semobjs[sem], amt)


def build(debug_dump=False):
    nc = bass.Bass("TRN2", target_bir_lowering=False, debug=False)

    x_in = nc.dram_tensor("x", [TC, D], F32, kind="ExternalInput").ap()
    wqkvu = nc.dram_tensor("wqkvu", [4 * D, D], F32, kind="ExternalInput").ap()
    wout = nc.dram_tensor("wout", [D, D], F32, kind="ExternalInput").ap()
    wg_i8 = nc.dram_tensor("wg", [D, INTER], I8, kind="ExternalInput").ap()
    wu_i8 = nc.dram_tensor("wu", [D, INTER], I8, kind="ExternalInput").ap()
    wd_i8 = nc.dram_tensor("wd", [INTER, D], I8, kind="ExternalInput").ap()
    sgate_in = nc.dram_tensor("sgate", [128, 64], F32, kind="ExternalInput").ap()
    sup_in = nc.dram_tensor("sup", [128, 64], F32, kind="ExternalInput").ap()
    sdown_in = nc.dram_tensor("sdown", [1, D], F32, kind="ExternalInput").ap()
    slope_in = nc.dram_tensor("slope", [1, H], F32, kind="ExternalInput").ap()
    mtidx_in = nc.dram_tensor("mtidx", [128, BLK], BF16, kind="ExternalInput").ap()
    arrq_in = nc.dram_tensor("arrq", [128, BLK], BF16, kind="ExternalInput").ap()
    arrkt_in = nc.dram_tensor("arrkt", [128, 2], F32, kind="ExternalInput").ap()
    arrcf_in = nc.dram_tensor("arrcf", [128, NBLK], F32, kind="ExternalInput").ap()
    arrbd_in = nc.dram_tensor("arrbd", [128, 1], F32, kind="ExternalInput").ap()
    iotacc_in = nc.dram_tensor("iotacc", [128, NCORES], F32, kind="ExternalInput").ap()
    idn_in = nc.dram_tensor("idn", [128, 128], BF16, kind="ExternalInput").ap()
    onesbf_in = nc.dram_tensor("onesbf", [128, 1], BF16, kind="ExternalInput").ap()
    onesf32_in = nc.dram_tensor("onesf32", [1, 128], F32, kind="ExternalInput").ap()
    cmisc_in = nc.dram_tensor("cmisc", [128, 6], F32, kind="ExternalInput").ap()

    y_out = nc.dram_tensor("y", [TC, D], F32, kind="ExternalOutput").ap()

    wq_bf = nc.dram_tensor("wq_bf", [4 * D, D], BF16, kind="Internal")
    wo_bf = nc.dram_tensor("wo_bf", [D, D], BF16, kind="Internal")
    x_bf = nc.dram_tensor("x_bf", [TC, D], BF16, kind="Internal")
    o_scr = nc.dram_tensor("o_scr", [H, 128, TC], F32, kind="Internal")
    u_scr = nc.dram_tensor("u_scr", [H, 128, TC], F32, kind="Internal")
    qdec_scr = nc.dram_tensor("qdec_scr", [H, 128, TC], BF16, kind="Internal")
    y1_scr = nc.dram_tensor("y1_scr", [TC, D], F32, kind="Internal")
    y1bf_scr = nc.dram_tensor("y1bf_scr", [TC, D], BF16, kind="Internal")
    cc_in = nc.dram_tensor("cc_in", [H * 128, HD], F32, kind="Internal")
    cc_out_a = nc.dram_tensor("cc_out_a", [NCORES * 8 * 128, HD], F32, kind="Internal")
    cc_out_b = nc.dram_tensor("cc_out_b", [NCORES * 8 * 128, HD], F32, kind="Internal")

    dbg = {}
    if debug_dump:
        for nm, shp, dt in [("dbg_o", [H, 128, TC], F32), ("dbg_u", [H, 128, TC], F32),
                            ("dbg_qdec", [H, 128, TC], F32), ("dbg_y1", [TC, D], F32),
                            ("dbg_kvi", [128, H * HD], F32),
                            ("dbg_y1pre", [TC, D], F32),
                            ("dbg_og", [N2A * 128, H * T2A], F32),
                            ("dbg_st", [N2A * 2, 128, 8], F32),
                            ("dbg_ofull", [N2A * 128, H * T2A], F32),
                            ("dbg_st2", [16, 128, 8], F32)]:
            dbg[nm] = nc.dram_tensor(nm, shp, dt, kind="ExternalOutput")

    b = B()

    es = ExitStack()
    with es:
        semobjs = {s: es.enter_context(nc.semaphore(s)) for s in SEMS}

        c_1bf = es.enter_context(nc.sbuf_tensor("c_1bf", [128, 1], BF16))
        c_1f32 = es.enter_context(nc.sbuf_tensor("c_1f32", [1, 128], F32))
        kvi_bf = es.enter_context(nc.sbuf_tensor("kvi_bf", [128, H * HD], BF16))
        c_cm = es.enter_context(nc.sbuf_tensor("c_cm", [128, 6], F32))

        arena = es.enter_context(nc.psum_tensor("arena", [128, 4096], F32))

        def bank(i, w=512):
            return arena[:, i * 512:i * 512 + w]

        def bank_bf(i):
            return arena[:, i * 512:(i + 1) * 512].bitcast(BF16)

        BANKS = tuple(f"b{i}" for i in range(8))

        b.E("scalar", lambda e: e.dma_start(out=c_cm[:, :], in_=cmisc_in),
            writes=("c_cm",), sem="dC", amt=16)
        def cast_x(i):
            b.E("gpsimd", (lambda i: lambda e: e.dma_start(
                out=x_bf[i * 1024:(i + 1) * 1024, :],
                in_=x_in[i * 1024:(i + 1) * 1024, :]))(i),
                writes=(f"x_bf{i}",), sem="dG", amt=16)

        def cast_wq(i):
            b.E("gpsimd", (lambda i: lambda e: e.dma_start(
                out=wq_bf[i * 1024:(i + 1) * 1024, :],
                in_=wqkvu[i * 1024:(i + 1) * 1024, :]))(i),
                writes=(f"wq_bf{i}",), sem="dG", amt=16)

        cast_x(0)
        cast_wq(4)
        cast_x(1)
        cast_wq(5)
        for i in (6, 7, 0, 1, 2, 3):
            cast_wq(i)
        for i in range(2):
            b.E("gpsimd", (lambda i: lambda e: e.dma_start(
                out=wo_bf[i * 1024:(i + 1) * 1024, :],
                in_=wout[i * 1024:(i + 1) * 1024, :]))(i),
                writes=(f"wo_bf{i}",), sem="dG", amt=16)
        WQALL = tuple(f"wq_bf{i}" for i in range(8))
        WOALL = ("wo_bf0", "wo_bf1")
        XBALL = ("x_bf0", "x_bf1")

        s1 = ExitStack()
        with s1:
            c_mtidx = s1.enter_context(nc.sbuf_tensor("c_mtidx", [128, BLK], BF16))
            c_arrq = s1.enter_context(nc.sbuf_tensor("c_arrq", [128, BLK], BF16))
            c_arrkt = s1.enter_context(nc.sbuf_tensor("c_arrkt", [128, 2], F32))
            c_arrcf = s1.enter_context(nc.sbuf_tensor("c_arrcf", [128, NBLK], F32))
            c_arrbd = s1.enter_context(nc.sbuf_tensor("c_arrbd", [128, 1], F32))
            c_iota = s1.enter_context(nc.sbuf_tensor("c_iota", [128, NCORES], F32))
            c_idn = s1.enter_context(nc.sbuf_tensor("c_idn", [128, 128], BF16))
            c_slope = s1.enter_context(nc.sbuf_tensor("c_slope", [1, H], F32))
            s_bc = s1.enter_context(nc.sbuf_tensor("s_bc", [128, H], F32))
            kvi_f32 = s1.enter_context(nc.sbuf_tensor("kvi_f32", [128, 2 * HD], F32))
            dfh = s1.enter_context(nc.sbuf_tensor("dfh", [128, NCORES], F32))
            for nm, dst, srcp in [("c_mtidx", c_mtidx, mtidx_in), ("c_arrq", c_arrq, arrq_in),
                                  ("c_arrkt", c_arrkt, arrkt_in), ("c_arrcf", c_arrcf, arrcf_in),
                                  ("c_arrbd", c_arrbd, arrbd_in), ("c_iota", c_iota, iotacc_in),
                                  ("c_idn", c_idn, idn_in), ("c_1bf", c_1bf, onesbf_in),
                                  ("c_1f32", c_1f32, onesf32_in), ("c_slope", c_slope, slope_in)]:
                b.E("scalar", (lambda d, s: lambda e: e.dma_start(out=d[:, :], in_=s))(dst, srcp),
                    writes=(nm,), sem="dC", amt=16)
            b.E("tensor", lambda e: nc.tensor.matmul(bank(0, H), c_1f32[0:1, :],
                                                     c_slope[0:1, :], start=True, stop=True),
                reads=("c_1f32", "c_slope"), writes=("b0",))
            b.E("scalar", lambda e: nc.scalar.copy(s_bc[:, :], bank(0, H)),
                reads=("b0",), writes=("s_bc",))
            b.E("scalar", lambda e: nc.scalar.copy(dfh[:, :], c_iota[:, :]),
                reads=("c_iota",), writes=("dfh",))
            b.E("scalar", lambda e: nc.scalar.copy(dfh[:, :], c_iota[:, :]),
                reads=("c_iota",), writes=("dfh",))

            mt = s1.enter_context(nc.sbuf_tensor("mt", [128, H * BLK], BF16))
            qdt = s1.enter_context(nc.sbuf_tensor("qdt", [128, H * BLK], BF16))
            kdc = s1.enter_context(nc.sbuf_tensor("kdc", [128, H * 2], F32))
            cft = s1.enter_context(nc.sbuf_tensor("cft", [128, H * NBLK], F32))
            bdc = s1.enter_context(nc.sbuf_tensor("bdc", [128, H], F32))
            x_feat = s1.enter_context(nc.sbuf_tensor("x_feat", [128, 16 * TC], BF16))
            v_tok = s1.enter_context(nc.sbuf_tensor("v_tok", [128, 16 * D], BF16))
            wt = s1.enter_context(nc.sbuf_tensor("wt", [128, 16 * 512], BF16))
            wt2 = s1.enter_context(nc.sbuf_tensor("wt2", [128, 16 * 512], BF16))
            wts = [wt, wt2]
            wtn = ["wt", "wt2"]
            q_bf = s1.enter_context(nc.sbuf_tensor("q_bf", [128, TC], BF16))
            k_bf = s1.enter_context(nc.sbuf_tensor("k_bf", [128, TC], BF16))
            qdb = s1.enter_context(nc.sbuf_tensor("qdb", [128, 2 * BLK], BF16))
            kqm = s1.enter_context(nc.sbuf_tensor("kqm", [128, 2 * (BLK + 128)], BF16))
            ktok = s1.enter_context(nc.sbuf_tensor("ktok", [128, TC], BF16))
            vdec = s1.enter_context(nc.sbuf_tensor("vdec", [128, 4 * 128], BF16))
            kv_sb = s1.enter_context(nc.sbuf_tensor("kv_sb", [128, HD], F32))
            kv_bfs = s1.enter_context(nc.sbuf_tensor("kv_bfs", [128, 2 * HD], BF16))
            qdec_sb = s1.enter_context(nc.sbuf_tensor("qdec_sb", [128, 2 * BLK], BF16))
            o_stage = s1.enter_context(nc.sbuf_tensor("o_stage", [128, 2 * BLK], F32))
            u_stage = s1.enter_context(nc.sbuf_tensor("u_stage", [128, 512], F32))

            for h in range(H):
                sb_h = s_bc[:, h:h + 1]
                b.E("scalar", (lambda h, sb_h: lambda e: nc.scalar.activation(
                    mt[:, h * BLK:(h + 1) * BLK], c_mtidx[:, :], AF.Exp, scale=sb_h))(h, sb_h),
                    reads=("c_mtidx", "s_bc"), writes=(f"mt{h}",), hard=True)
                b.E("scalar", (lambda h, sb_h: lambda e: nc.scalar.activation(
                    qdt[:, h * BLK:(h + 1) * BLK], c_arrq[:, :], AF.Exp, scale=sb_h))(h, sb_h),
                    reads=("c_arrq", "s_bc"), writes=(f"qdt{h}",), hard=True)
                b.E("scalar", (lambda h, sb_h: lambda e: nc.scalar.activation(
                    kdc[:, h * 2:(h + 1) * 2], c_arrkt[:, :], AF.Exp, scale=sb_h))(h, sb_h),
                    reads=("c_arrkt", "s_bc"), writes=(f"kdc{h}",), hard=True)
                b.E("scalar", (lambda h, sb_h: lambda e: nc.scalar.activation(
                    cft[:, h * NBLK:(h + 1) * NBLK], c_arrcf[:, :], AF.Exp, scale=sb_h))(h, sb_h),
                    reads=("c_arrcf", "s_bc"), writes=(f"cft{h}",), hard=True)
                b.E("scalar", (lambda h, sb_h: lambda e: nc.scalar.activation(
                    bdc[:, h:h + 1], c_arrbd[:, :], AF.Exp, scale=sb_h))(h, sb_h),
                    reads=("c_arrbd", "s_bc"), writes=(f"bdc{h}",), hard=True)

            def emit_vwt_loads(q):
                wb = wts[q % 2]
                wbn = wtn[q % 2]
                b.G("sync", [
                    (lambda q, ec, wb: lambda e: e.dma_start(
                        out=wb[:, ec * 512:(ec + 1) * 512],
                        in_=wq_bf[2 * D + q * 512:2 * D + (q + 1) * 512,
                                  ec * 128:(ec + 1) * 128], transpose=True))(q, ec, wb)
                    for ec in range(16)],
                    reads=(f"wq_bf{4 + q // 2}",), writes=(wbn,), sem="dS")

            def emit_xfeat(half):
                b.G("sync", [
                    (lambda ec, half: lambda e: e.dma_start(
                        out=x_feat[:, ec * TC + half * 1024: ec * TC + (half + 1) * 1024],
                        in_=x_bf[half * 1024:(half + 1) * 1024,
                                 ec * 128:(ec + 1) * 128], transpose=True))(ec, half)
                    for ec in range(16)],
                    reads=(f"x_bf{half}",), writes=(f"x_feat{half}",), sem="dS")

            emit_xfeat(0)
            emit_vwt_loads(0)
            emit_xfeat(1)

            for q in range(4):
                wb = wts[q % 2]
                wbn = wtn[q % 2]
                if q > 0:
                    emit_vwt_loads(q)
                for tc in range(16):
                    bk = tc % 2
                    b.G("tensor", [
                        (lambda tc, ec, bk, wb: lambda e: nc.tensor.matmul(
                            bank(bk),
                            x_feat[:, ec * TC + tc * 128: ec * TC + (tc + 1) * 128],
                            wb[:, ec * 512:(ec + 1) * 512],
                            start=(ec == 0), stop=(ec == 15)))(tc, ec, bk, wb)
                        for ec in range(16)],
                        reads=(f"x_feat{tc // 8}", wbn), writes=(f"b{bk}",))
                    b.E("vector", (lambda q, tc, bk: lambda e: nc.vector.tensor_copy(
                        v_tok[:, tc * D + q * 512: tc * D + (q + 1) * 512], bank(bk)))(q, tc, bk),
                        reads=(f"b{bk}",), writes=("v_tok",))

            for q in range(4):
                wb = wts[q % 2]
                wbn = wtn[q % 2]
                b.G("sync", [
                    (lambda q, ec, wb: lambda e: e.dma_start(
                        out=wb[:, ec * 512:(ec + 1) * 512],
                        in_=wq_bf[3 * D + q * 512:3 * D + (q + 1) * 512,
                                  ec * 128:(ec + 1) * 128], transpose=True))(q, ec, wb)
                    for ec in range(16)],
                    reads=(f"wq_bf{6 + q // 2}",), writes=(wbn,), sem="dS")
                for fbq in range(4):
                    fb = q * 4 + fbq
                    for ts in range(4):
                        bk = ts % 2
                        b.G("tensor", [
                            (lambda fbq, ts, ec, bk, wb: lambda e: nc.tensor.matmul(
                                bank(bk),
                                wb[:, ec * 512 + fbq * 128: ec * 512 + (fbq + 1) * 128],
                                x_feat[:, ec * TC + ts * 512: ec * TC + (ts + 1) * 512],
                                start=(ec == 0), stop=(ec == 15)))(fbq, ts, ec, bk, wb)
                            for ec in range(16)],
                            reads=(f"x_feat{ts // 2}", wbn), writes=(f"b{bk}",))
                        b.E("scalar", (lambda ts, bk: lambda e: nc.scalar.copy(
                            u_stage[:, :], bank(bk)))(ts, bk),
                            reads=(f"b{bk}",), writes=("u_st",))
                        b.E("scalar", (lambda fb, ts: lambda e: e.dma_start(
                            out=u_scr.ap()[fb, :, ts * 512:(ts + 1) * 512],
                            in_=u_stage[:, :]))(fb, ts),
                            reads=("u_st",), sem="dC", amt=16)

            for g in range(8):
                wb = wts[g % 2]
                wbn = wtn[g % 2]
                b.G("sync", [f for ec in range(16) for f in (
                    (lambda g, ec, wb: lambda e: e.dma_start(
                        out=wb[:, ec * 512:ec * 512 + 256],
                        in_=wq_bf[g * 256:(g + 1) * 256, ec * 128:(ec + 1) * 128],
                        transpose=True))(g, ec, wb),
                    (lambda g, ec, wb: lambda e: e.dma_start(
                        out=wb[:, ec * 512 + 256:(ec + 1) * 512],
                        in_=wq_bf[D + g * 256:D + (g + 1) * 256, ec * 128:(ec + 1) * 128],
                        transpose=True))(g, ec, wb))],
                    reads=(f"wq_bf{g // 4}", f"wq_bf{2 + g // 4}"), writes=(wbn,), sem="dS")

                for hh in range(2):
                    h = 2 * g + hh
                    for which, dst, dstn in (("q", q_bf, "q_bf"), ("k", k_bf, "k_bf")):
                        coloff = hh * 128 if which == "q" else 256 + hh * 128
                        for ts in range(4):
                            bk = ts
                            b.G("tensor", [
                                (lambda coloff, ts, ec, bk, wb: lambda e: nc.tensor.matmul(
                                    bank(bk),
                                    wb[:, ec * 512 + coloff: ec * 512 + coloff + 128],
                                    x_feat[:, ec * TC + ts * 512: ec * TC + (ts + 1) * 512],
                                    start=(ec == 0), stop=(ec == 15)))(coloff, ts, ec, bk, wb)
                                for ec in range(16)],
                                reads=(f"x_feat{ts // 2}", wbn), writes=(f"b{bk}",))
                            b.E("scalar", (lambda dst, ts, bk: lambda e: nc.scalar.activation(
                                dst[:, ts * 512:(ts + 1) * 512], bank(bk), AF.Silu))(dst, ts, bk),
                                reads=(f"b{bk}",), writes=(dstn,))

                    for tsb in range(16):
                        tb = 2 + tsb % 2
                        b.E("tensor", (lambda tsb, tb: lambda e: nc.tensor.transpose(
                            bank_bf(tb)[:, 0:128],
                            k_bf[:, tsb * 128:(tsb + 1) * 128],
                            c_idn[:, :]))(tsb, tb),
                            reads=("k_bf", "c_idn"), writes=(f"b{tb}",))
                        b.E("scalar", (lambda tsb, tb: lambda e: nc.scalar.copy(
                            ktok[:, tsb * 128:(tsb + 1) * 128], bank_bf(tb)[:, 0:128]))(tsb, tb),
                            reads=(f"b{tb}",), writes=("ktok",))

                    def front(h, j):
                        pj = j % 2
                        qo = pj * BLK
                        ko = pj * (BLK + 128)
                        vo = pj * 256
                        b.E("vector", (lambda h, j, qo: lambda e: nc.vector.tensor_tensor(
                            qdb[:, qo:qo + BLK], q_bf[:, j * BLK:(j + 1) * BLK],
                            qdt[:, h * BLK:(h + 1) * BLK], ALU.mult))(h, j, qo),
                            reads=("q_bf", f"qdt{h}"), writes=(f"qdb{pj}",))
                        b.E("vector", (lambda h, j, qo, pj: lambda e: nc.vector.tensor_scalar_mul(
                            qdec_sb[:, pj * BLK:(pj + 1) * BLK], qdb[:, qo:qo + BLK],
                            cft[:, h * NBLK + j: h * NBLK + j + 1]))(h, j, qo, pj),
                            reads=(f"qdb{pj}", f"cft{h}"), writes=(f"qds{pj}",))
                        b.E("scalar", (lambda h, j, pj: lambda e: e.dma_start(
                            out=qdec_scr.ap()[h, :, j * BLK:(j + 1) * BLK],
                            in_=qdec_sb[:, pj * BLK:(pj + 1) * BLK]))(h, j, pj),
                            reads=(f"qds{pj}",), sem="dC", amt=16)
                        for s in (0, 1):
                            bk = 4 + s
                            nq = BLK if s == 0 else 128
                            qoff = 0 if s == 0 else 128
                            b.E("tensor", (lambda h, j, s, bk, nq, qoff: lambda e: nc.tensor.matmul(
                                bank(bk, nq),
                                k_bf[:, j * BLK + s * 128: j * BLK + s * 128 + 128],
                                q_bf[:, j * BLK + qoff: j * BLK + qoff + nq],
                                start=True, stop=True))(h, j, s, bk, nq, qoff),
                                reads=("k_bf", "q_bf"), writes=(f"b{bk}",))
                            b.E("vector", (lambda h, j, s, bk, nq, ko: lambda e: nc.vector.tensor_tensor(
                                kqm[:, ko + s * BLK: ko + s * BLK + nq], bank(bk, nq),
                                mt[:, h * BLK: h * BLK + nq],
                                ALU.mult))(h, j, s, bk, nq, ko),
                                reads=(f"b{bk}", f"mt{h}"), writes=(f"kqm{pj}",))
                            b.E("vector", (lambda h, j, s, vo: lambda e: nc.vector.tensor_scalar_mul(
                                vdec[:, vo + s * 128: vo + (s + 1) * 128],
                                v_tok[:, (2 * j + s) * D + h * 128: (2 * j + s) * D + (h + 1) * 128],
                                kdc[:, h * 2 + s: h * 2 + s + 1]))(h, j, s, vo),
                                reads=("v_tok", f"kdc{h}"), writes=(f"vdec{pj}",))

                    front(h, 0)
                    for j in range(NBLK):
                        pj = j % 2
                        qo = pj * BLK
                        ko = pj * (BLK + 128)
                        vo = pj * 256
                        if j + 1 < NBLK:
                            front(h, j + 1)

                        b.G("tensor", [
                            (lambda j, s, vo: lambda e: nc.tensor.matmul(
                                bank(7, HD), ktok[:, (2 * j + s) * 128:(2 * j + s + 1) * 128],
                                vdec[:, vo + s * 128: vo + (s + 1) * 128],
                                start=(s == 0), stop=(s == 1)))(j, s, vo)
                            for s in (0, 1)],
                            reads=("ktok", f"vdec{pj}"), writes=("b7",))

                        fns = []
                        o_reads = [f"kqm{pj}", "v_tok"]
                        fns.append((lambda h, j, ko: lambda e: nc.tensor.matmul(
                            bank(6, BLK),
                            v_tok[:, (2 * j) * D + h * 128: (2 * j) * D + (h + 1) * 128],
                            kqm[:, ko:ko + BLK],
                            start=True, stop=False))(h, j, ko))
                        fns.append((lambda h, j, ko: lambda e: nc.tensor.matmul(
                            arena[:, 6 * 512 + 128: 6 * 512 + 256],
                            v_tok[:, (2 * j + 1) * D + h * 128: (2 * j + 1) * D + (h + 1) * 128],
                            kqm[:, ko + BLK: ko + BLK + 128],
                            start=False, stop=(j == 0), skip_group_check=True))(h, j, ko))
                        if j > 0:
                            o_reads += [f"kv_bfs{(j - 1) % 2}", f"qdb{pj}"]
                            fns.append((lambda qo, j: lambda e: nc.tensor.matmul(
                                bank(6, BLK),
                                kv_bfs[:, ((j - 1) % 2) * HD:((j - 1) % 2 + 1) * HD],
                                qdb[:, qo:qo + BLK],
                                start=False, stop=True, skip_group_check=True))(qo, j))
                        b.G("tensor", fns, reads=tuple(o_reads), writes=("b6",))

                        par = j % 2
                        b.E("scalar", (lambda par: lambda e: nc.scalar.copy(
                            o_stage[:, par * BLK:(par + 1) * BLK], bank(6, BLK)))(par),
                            reads=("b6",), writes=(f"o_st{par}",))
                        b.E("scalar", (lambda h, j, par: lambda e: e.dma_start(
                            out=o_scr.ap()[h, :, j * BLK:(j + 1) * BLK],
                            in_=o_stage[:, par * BLK:(par + 1) * BLK]))(h, j, par),
                            reads=(f"o_st{par}",), sem="dC", amt=16)

                        if j == 0:
                            b.E("vector", lambda e: nc.vector.tensor_copy(
                                kv_sb[:, :], bank(7, HD)),
                                reads=("b7",), writes=("kv_sb",))
                        else:
                            b.E("vector", (lambda h: lambda e: nc.vector.scalar_tensor_tensor(
                                kv_sb[:, :], kv_sb[:, :], bdc[:, h:h + 1], bank(7, HD),
                                ALU.mult, ALU.add))(h),
                                reads=(f"bdc{h}", "b7", "kv_sb"), writes=("kv_sb",))
                        b.E("scalar", (lambda pj: lambda e: nc.scalar.copy(
                            kv_bfs[:, pj * HD:(pj + 1) * HD], kv_sb[:, :]))(pj),
                            reads=("kv_sb",), writes=(f"kv_bfs{pj}",))

                    b.E("scalar", (lambda h: lambda e: e.dma_start(
                        out=cc_in[h * 128:(h + 1) * 128, :], in_=kv_sb[:, :]))(h),
                        reads=("kv_sb",), writes=(f"cc_in{h}",), sem="dC", amt=16)


                if g == 3:
                    b.E("gpsimd", lambda e: e.collective_compute(
                        "AllGather", ALU.bypass, replica_groups=[list(range(NCORES))],
                        ins=[cc_in.ap()[0:8 * 128, :].opt()], outs=[cc_out_a.ap().opt()]),
                        reads=tuple(f"cc_in{h}" for h in range(8)), writes=("cc_out_a",),
                        sem="cc", amt=1)

            b.E("gpsimd", lambda e: e.collective_compute(
                "AllGather", ALU.bypass, replica_groups=[list(range(NCORES))],
                ins=[cc_in.ap()[8 * 128:H * 128, :].opt()], outs=[cc_out_b.ap().opt()]),
                reads=tuple(f"cc_in{h}" for h in range(8, H)), writes=("cc_out_b",),
                sem="cc", amt=1)

            a_alls = [wt[:, 0:8192].bitcast(F32), wt2[:, 0:8192].bitcast(F32)]
            for hq in range(4):
                a_all = a_alls[hq % 2]
                abn = wtn[hq % 2]
                cc_half = cc_out_a if hq < 2 else cc_out_b
                ccn = "cc_out_a" if hq < 2 else "cc_out_b"
                grp = []
                for hh in range(4):
                    h = hq * 4 + hh
                    src = cc_half.ap().rearrange("(r x) w -> r x w", x=8 * 128)[
                        :, (h % 8) * 128:((h % 8) + 1) * 128, :].rearrange("r p w -> p r w")
                    dst = a_all[:, hh * 1024:(hh + 1) * 1024].rearrange(
                        "p (r w) -> p r w", r=NCORES)
                    grp.append((lambda dst, src: lambda e: e.dma_start(out=dst, in_=src))(dst, src))
                b.G("sync", grp, reads=(ccn,), writes=(abn,), sem="dS")
                for hh in range(4):
                    h = hq * 4 + hh
                    b.E("scalar", (lambda h: lambda e: nc.scalar.activation(
                        dfh[:, :], c_iota[:, :], AF.Exp, scale=s_bc[:, h:h + 1]))(h),
                        reads=("c_iota", "s_bc"), writes=("dfh",), hard=True)
                    b.E("vector", (lambda hh, a_all: lambda e: nc.vector.tensor_scalar_mul(
                        kvi_f32[:, 0:HD], a_all[:, hh * 1024: hh * 1024 + 128],
                        dfh[:, 0:1]))(hh, a_all),
                        reads=(abn, "dfh"), writes=("kvi0",))
                    pp = 0
                    for r in range(1, NCORES):
                        src_col = hh * 1024 + r * 128
                        b.E("vector", (lambda r, pp, src_col, a_all: lambda e: nc.vector.scalar_tensor_tensor(
                            kvi_f32[:, (1 - pp) * HD:(2 - pp) * HD],
                            a_all[:, src_col:src_col + 128],
                            dfh[:, r:r + 1],
                            kvi_f32[:, pp * HD:(pp + 1) * HD],
                            ALU.mult, ALU.add))(r, pp, src_col, a_all),
                            reads=(abn, "dfh", f"kvi{pp}"), writes=(f"kvi{1 - pp}",))
                        pp = 1 - pp
                    b.E("vector", (lambda h, pp: lambda e: nc.vector.tensor_copy(
                        kvi_bf[:, h * HD:(h + 1) * HD],
                        kvi_f32[:, pp * HD:(pp + 1) * HD]))(h, pp),
                        reads=(f"kvi{pp}",), writes=("kvi_bf",))

            if debug_dump:
                b.E("gpsimd", lambda e: e.dma_start(
                    out=dbg["dbg_kvi"].ap()[:, :],
                    in_=kvi_bf[:, :]), reads=("kvi_bf",), sem="dG", amt=16)

        marks_cc = b.barrier_all()

        s2a = ExitStack()
        with s2a:
            woutT = s2a.enter_context(nc.sbuf_tensor("woutT", [128, 16 * D], BF16))
            o_sb = s2a.enter_context(nc.sbuf_tensor("o_sb", [128, 2 * H * T2A], F32))
            u_sb = s2a.enter_context(nc.sbuf_tensor("u_sb", [128, 2 * H * T2A], F32))
            qdec2 = s2a.enter_context(nc.sbuf_tensor("qdec2", [128, 2 * H * T2A], BF16))
            x_sb = s2a.enter_context(nc.sbuf_tensor("x_sb", [128, 2 * D], F32))
            og_sb = s2a.enter_context(nc.sbuf_tensor("og_sb", [128, H * T2A], BF16))
            sq_sb = s2a.enter_context(nc.sbuf_tensor("sq_sb", [128, 3 * D], BF16))
            tmp_f = s2a.enter_context(nc.sbuf_tensor("tmp_f", [128, T2A], F32))
            row_f = s2a.enter_context(nc.sbuf_tensor("row_f", [1, 2 * T2A], F32))
            y1_sb = s2a.enter_context(nc.sbuf_tensor("y1_sb", [128, D], F32))
            y1n_sb = s2a.enter_context(nc.sbuf_tensor("y1n_sb", [128, D], F32))
            y1nbf = s2a.enter_context(nc.sbuf_tensor("y1nbf", [128, D], BF16))
            st_sb = s2a.enter_context(nc.sbuf_tensor("st_sb", [128, 9], F32))

            b.G("scalar", [
                (lambda ec: lambda e: e.dma_start(
                    out=woutT[:, ec * D:(ec + 1) * D],
                    in_=wo_bf[:, ec * 128:(ec + 1) * 128], transpose=True))(ec)
                for ec in range(16)],
                reads=WOALL, writes=("woutT",), sem="dC")

            HT = H * T2A

            def emit_scratch_loads(ti):
                t0 = ti * T2A
                par = ti % 2
                for half in range(2):
                    for scr, dstbuf, dn in ((o_scr, o_sb, "o_sb"), (u_scr, u_sb, "u_sb"),
                                            (qdec_scr, qdec2, "qdec2")):
                        src = scr.ap()[half * 8:(half + 1) * 8, :, t0:t0 + T2A].rearrange(
                            "h p t -> p h t")
                        dst = dstbuf[:, par * HT + half * 8 * T2A:
                                     par * HT + (half + 1) * 8 * T2A].rearrange(
                            "p (h t) -> p h t", h=8)
                        b.E("sync", (lambda dst, src: lambda e: e.dma_start(
                            out=dst, in_=src))(dst, src), writes=(f"{dn}{par}",),
                            sem="dS", amt=16)

            def stageA(ti):
                par = ti % 2
                pb = par * HT
                for h in range(H):
                    cb = 0 if h % 2 == 0 else 3
                    b.E("tensor", (lambda h, pb, cb: lambda e: nc.tensor.matmul(
                        bank(cb, T2A), kvi_bf[:, h * HD:(h + 1) * HD],
                        qdec2[:, pb + h * T2A: pb + (h + 1) * T2A], start=True, stop=True))(h, pb, cb),
                        reads=("kvi_bf", f"qdec2{par}"), writes=(f"b{cb}",))
                    b.E("vector", (lambda h, pb, cb: lambda e: nc.vector.tensor_tensor(
                        o_sb[:, pb + h * T2A: pb + (h + 1) * T2A],
                        o_sb[:, pb + h * T2A: pb + (h + 1) * T2A],
                        bank(cb, T2A), ALU.add))(h, pb, cb),
                        reads=(f"b{cb}", f"o_sb{par}"), writes=(f"o_sb{par}",))

            emit_scratch_loads(0)
            emit_scratch_loads(1)
            stageA(0)
            for ti in range(N2A):
                t0 = ti * T2A
                par = ti % 2
                pb = par * HT
                src = x_in[t0:t0 + T2A, :].rearrange("(tc p) e -> p tc e", p=128)
                dst = x_sb[:, :].rearrange("p (tc e) -> p tc e", tc=2)
                b.E("sync", (lambda dst, src: lambda e: e.dma_start(
                    out=dst, in_=src))(dst, src), writes=("x_sb",), sem="dS", amt=16)

                b.E("vector", (lambda pb: lambda e: nc.vector.tensor_tensor(
                    sq_sb[:, 0:H * T2A], o_sb[:, pb:pb + HT], o_sb[:, pb:pb + HT],
                    ALU.mult))(pb),
                    reads=(f"o_sb{par}",), writes=("sq_sb",))
                b.G("tensor", [
                    (lambda h: lambda e: nc.tensor.matmul(
                        bank(1, T2A)[0:1, :], c_1bf[:, :], sq_sb[:, h * T2A:(h + 1) * T2A],
                        start=(h == 0), stop=(h == 15)))(h)
                    for h in range(H)],
                    reads=("c_1bf", "sq_sb"), writes=("b1",))
                b.E("scalar", lambda e: nc.scalar.activation(
                    row_f[0:1, 0:T2A], bank(1, T2A)[0:1, :], AF.Ln,
                    scale=c_cm[0:1, 0:1], bias=c_cm[0:1, 1:2]),
                    reads=("b1", "c_cm"), writes=("row0",), hard=True)
                b.E("scalar", lambda e: nc.scalar.activation(
                    row_f[0:1, T2A:2 * T2A], row_f[0:1, 0:T2A], AF.Exp,
                    scale=c_cm[0:1, 3:4]),
                    reads=("row0", "c_cm"), writes=("row1",), hard=True)
                if ti + 1 < N2A:
                    stageA(ti + 1)
                b.E("tensor", lambda e: nc.tensor.matmul(
                    bank(2, T2A), c_1f32[0:1, :], row_f[0:1, T2A:2 * T2A],
                    start=True, stop=True),
                    reads=("c_1f32", "row1"), writes=("b2",))

                for h in range(H):
                    b.E("vector", (lambda h, pb: lambda e: nc.vector.tensor_tensor(
                        tmp_f[:, :], o_sb[:, pb + h * T2A: pb + (h + 1) * T2A],
                        bank(2, T2A), ALU.mult))(h, pb),
                        reads=(f"o_sb{par}", "b2"), writes=("tmp_f",))
                    b.E("vector", (lambda h, pb: lambda e: nc.vector.tensor_tensor(
                        og_sb[:, h * T2A:(h + 1) * T2A], tmp_f[:, :],
                        u_sb[:, pb + h * T2A: pb + (h + 1) * T2A], ALU.mult))(h, pb),
                        reads=("tmp_f", f"u_sb{par}"), writes=(f"og{h}",))

                if ti + 2 < N2A:
                    emit_scratch_loads(ti + 2)

                if debug_dump:
                    b.E("gpsimd", (lambda ti: lambda e: e.dma_start(
                        out=dbg["dbg_og"].ap()[ti * 128:(ti + 1) * 128, :],
                        in_=og_sb[:, :]))(ti),
                        reads=("og_sb",), sem="dG", amt=16)
                    b.E("sync", (lambda ti, pb: lambda e: e.dma_start(
                        out=dbg["dbg_ofull"].ap()[ti * 128:(ti + 1) * 128, :],
                        in_=o_sb[:, pb:pb + HT]))(ti, pb),
                        reads=(f"o_sb{par}",), sem="dS", amt=16)
                for tc2 in range(2):
                    for h in range(H):
                        b.G("tensor", [
                            (lambda tc2, ns, h: lambda e: nc.tensor.matmul(
                                arena[:, (4 + ns) * 512:(5 + ns) * 512],
                                og_sb[:, h * T2A + tc2 * 128: h * T2A + tc2 * 128 + 128],
                                woutT[:, h * D + ns * 512: h * D + (ns + 1) * 512],
                                start=(h == 0), stop=(h == 15),
                                skip_group_check=True))(tc2, ns, h)
                            for ns in range(4)],
                            reads=(f"og{h}", "woutT"), writes=("b4", "b5", "b6", "b7"))
                    b.E("vector", (lambda tc2: lambda e: nc.vector.tensor_tensor(
                        y1_sb[:, :], arena[:, 4 * 512: 8 * 512],
                        x_sb[:, tc2 * D:(tc2 + 1) * D], ALU.add))(tc2),
                        reads=("b4", "b5", "b6", "b7", "x_sb"), writes=("y1_sb",))
                    if debug_dump:
                        b.E("sync", (lambda t0, tc2: lambda e: e.dma_start(
                            out=dbg["dbg_y1pre"].ap()[t0 + tc2 * 128: t0 + (tc2 + 1) * 128, :],
                            in_=y1_sb[:, :]))(t0, tc2),
                            reads=("y1_sb",), sem="dS", amt=16)
                    b.E("scalar", lambda e: nc.scalar.activation(
                        sq_sb[:, 0:D], y1_sb[:, :], AF.Copy, accum_out=st_sb[:, 0:1]),
                        reads=("y1_sb",), writes=("sq_sb", "st0"))
                    b.E("vector", lambda e: nc.vector.tensor_tensor(
                        sq_sb[:, D:2 * D], y1_sb[:, :], y1_sb[:, :], ALU.mult),
                        reads=("y1_sb",), writes=("sq_sb",))
                    b.E("scalar", lambda e: nc.scalar.activation(
                        sq_sb[:, 2 * D:3 * D], sq_sb[:, D:2 * D], AF.Copy,
                        accum_out=st_sb[:, 1:2]),
                        reads=("sq_sb",), writes=("sq_sb", "st1"))
                    b.E("scalar", lambda e: nc.scalar.mul(
                        st_sb[:, 2:3], st_sb[:, 0:1], c_cm[:, 0:1]),
                        reads=("st0", "c_cm"), writes=("st2",), hard=True)
                    b.E("scalar", lambda e: nc.scalar.mul(
                        st_sb[:, 3:4], st_sb[:, 1:2], c_cm[:, 0:1]),
                        reads=("st1", "c_cm"), writes=("st3",), hard=True)
                    b.E("scalar", lambda e: nc.scalar.mul(
                        st_sb[:, 4:5], st_sb[:, 2:3], st_sb[:, 2:3]),
                        reads=("st2",), writes=("st4",), hard=True)
                    b.E("scalar", lambda e: nc.scalar.mul(
                        st_sb[:, 8:9], st_sb[:, 0:1], c_cm[:, 0:1]),
                        reads=("st0", "c_cm"), writes=("st8",), hard=True)
                    b.E("scalar", lambda e: nc.scalar.activation(
                        st_sb[:, 5:6], st_sb[:, 4:5], AF.Identity,
                        scale=c_cm[:, 4:5], bias=st_sb[:, 3:4]),
                        reads=("st3", "st4", "c_cm"), writes=("st5",), hard=True)
                    b.E("scalar", lambda e: nc.scalar.activation(
                        st_sb[:, 6:7], st_sb[:, 5:6], AF.Ln, bias=c_cm[:, 2:3]),
                        reads=("st5", "c_cm"), writes=("st6",), hard=True)
                    b.E("scalar", lambda e: nc.scalar.activation(
                        st_sb[:, 6:7], st_sb[:, 6:7], AF.Exp, scale=c_cm[:, 3:4]),
                        reads=("st6", "c_cm"), writes=("st6",), hard=True)
                    b.E("scalar", lambda e: nc.scalar.mul(
                        st_sb[:, 8:9], st_sb[:, 0:1], c_cm[:, 0:1]),
                        reads=("st0", "c_cm"), writes=("st8",), hard=True)
                    b.E("scalar", lambda e: nc.scalar.mul(
                        st_sb[:, 7:8], st_sb[:, 6:7], st_sb[:, 2:3]),
                        reads=("st2", "st6"), writes=("st7",), hard=True)
                    b.E("scalar", lambda e: nc.scalar.mul(
                        st_sb[:, 7:8], st_sb[:, 7:8], c_cm[:, 4:5]),
                        reads=("st7", "c_cm"), writes=("st7",), hard=True)
                    b.E("scalar", lambda e: nc.scalar.mul(
                        st_sb[:, 8:9], st_sb[:, 0:1], c_cm[:, 0:1]),
                        reads=("st0", "c_cm"), writes=("st8",), hard=True)
                    b.E("scalar", lambda e: nc.scalar.activation(
                        y1n_sb[:, :], y1_sb[:, :], AF.Identity,
                        scale=st_sb[:, 6:7], bias=st_sb[:, 7:8]),
                        reads=("y1_sb", "st6", "st7"), writes=("y1n_sb",), hard=True)
                    if debug_dump:
                        b.E("sync", (lambda ti, tc2: lambda e: e.dma_start(
                            out=dbg["dbg_st"].ap()[ti * 2 + tc2, :, :],
                            in_=st_sb[:, 0:8]))(ti, tc2),
                            reads=("st0", "st1", "st2", "st3", "st4", "st5", "st6", "st7"),
                            sem="dS", amt=16)
                    b.E("vector", lambda e: nc.vector.tensor_copy(y1nbf[:, :], y1n_sb[:, :]),
                        reads=("y1n_sb",), writes=("y1nbf",))
                    b.E("scalar", (lambda t0, tc2: lambda e: e.dma_start(
                        out=y1_scr[t0 + tc2 * 128: t0 + (tc2 + 1) * 128, :],
                        in_=y1n_sb[:, :]))(t0, tc2),
                        reads=("y1n_sb",), sem="dC", amt=16)
                    b.E("scalar", (lambda t0, tc2: lambda e: e.dma_start(
                        out=y1bf_scr[t0 + tc2 * 128: t0 + (tc2 + 1) * 128, :],
                        in_=y1nbf[:, :]))(t0, tc2),
                        reads=("y1nbf",), sem="dC", amt=16)

        marks_2a = b.barrier_all()

        if debug_dump:
            for nm, scr in [("dbg_o", o_scr), ("dbg_u", u_scr)]:
                for h in range(H):
                    b.E("sync", (lambda nm, scr, h: lambda e: e.dma_start(
                        out=dbg[nm].ap()[h, :, :], in_=scr.ap()[h, :, :]))(nm, scr, h),
                        sem="dS", amt=16)
            for h in range(H):
                b.E("gpsimd", (lambda h: lambda e: e.dma_start(
                    out=dbg["dbg_qdec"].ap()[h, :, :], in_=qdec_scr.ap()[h, :, :]))(h),
                    sem="dG", amt=16)
            for i in range(4):
                b.E("sync", (lambda i: lambda e: e.dma_start(
                    out=dbg["dbg_y1"].ap()[i * 512:(i + 1) * 512, :],
                    in_=y1_scr[i * 512:(i + 1) * 512, :]))(i),
                    sem="dS", amt=16)

        s2b = ExitStack()
        with s2b:
            y1f = s2b.enter_context(nc.sbuf_tensor("y1f", [128, 16 * T2B], BF16))
            fuse = s2b.enter_context(nc.sbuf_tensor("fuse", [128, 64 * T2B], BF16))
            wgu = s2b.enter_context(nc.sbuf_tensor("wgu", [128, 2 * 16384], BF16))
            wd_sb = s2b.enter_context(nc.sbuf_tensor("wd_sb", [128, 6 * D], BF16))
            sg_sb = s2b.enter_context(nc.sbuf_tensor("sg_sb", [128, 64], F32))
            su_sb = s2b.enter_context(nc.sbuf_tensor("su_sb", [128, 64], F32))
            sdb = s2b.enter_context(nc.sbuf_tensor("sdb", [128, D], F32))
            gs_sb = s2b.enter_context(nc.sbuf_tensor("gs_sb", [128, T2B], BF16))
            y1tok = s2b.enter_context(nc.sbuf_tensor("y1tok", [128, D], F32))
            y2n_sb = s2b.enter_context(nc.sbuf_tensor("y2n_sb", [128, D], F32))
            st2 = s2b.enter_context(nc.sbuf_tensor("st2", [128, 9], F32))
            dum2 = s2b.enter_context(nc.sbuf_tensor("dum2", [128, 2 * D], BF16))

            b.E("sync", lambda e: e.dma_start(out=sg_sb[:, :], in_=sgate_in),
                writes=("sg_sb",), sem="dS", amt=16)
            b.E("sync", lambda e: e.dma_start(out=su_sb[:, :], in_=sup_in),
                writes=("su_sb",), sem="dS", amt=16)
            b.E("sync", lambda e: e.dma_start(out=y1tok[0:1, 0:D], in_=sdown_in),
                writes=("sdr",), sem="dS", amt=16)
            b.G("tensor", [
                (lambda ns: lambda e: nc.tensor.matmul(
                    arena[:, (4 + ns) * 512:(5 + ns) * 512], c_1f32[0:1, :],
                    y1tok[0:1, ns * 512:(ns + 1) * 512], start=True, stop=True))(ns)
                for ns in range(4)],
                reads=("c_1f32", "sdr"), writes=("b4", "b5", "b6", "b7"))
            b.E("scalar", lambda e: nc.scalar.copy(sdb[:, :], arena[:, 2048:4096]),
                reads=("b4", "b5", "b6", "b7"), writes=("sdb",))

            def emit_y1f_loads(ti):
                t0 = ti * T2B
                b.G("sync", [
                    (lambda t0, hc: lambda e: e.dma_start(
                        out=y1f[:, hc * T2B:(hc + 1) * T2B],
                        in_=y1bf_scr[t0:t0 + T2B, hc * 128:(hc + 1) * 128],
                        transpose=True))(t0, hc)
                    for hc in range(16)],
                    writes=("y1f",), sem="dS")

            def emit_wgu_loads(ti, ig):
                pb = ig % 2
                base = pb * 16384
                grp = []
                for half in range(2):
                    for mat, off in ((wg_i8, base), (wu_i8, base + 8192)):
                        srcp = mat[half * 1024:(half + 1) * 1024,
                                   ig * 512:(ig + 1) * 512].rearrange(
                            "(hc p) i -> p hc i", p=128)
                        dst = wgu[:, off + half * 8 * 512: off + (half + 1) * 8 * 512].rearrange(
                            "p (hc i) -> p hc i", hc=8)
                        grp.append((lambda dst, srcp: lambda e: e.dma_start(
                            out=dst, in_=srcp))(dst, srcp))
                b.G("gpsimd", grp, writes=(f"wgu{pb}",), sem="dG")

            loaded = set()
            emit_y1f_loads(0)
            for ti in range(N2B):
                t0 = ti * T2B

                for ig in range(16):
                    pb = ig % 2
                    base = pb * 16384
                    if (ti, ig) not in loaded:
                        emit_wgu_loads(ti, ig)
                        loaded.add((ti, ig))

                    for ic4 in range(4):
                        b.G("tensor", [
                            (lambda base, ic4, hc: lambda e: nc.tensor.matmul(
                                bank(ic4),
                                wgu[:, base + hc * 512 + ic4 * 128: base + hc * 512 + ic4 * 128 + 128],
                                y1f[:, hc * T2B:(hc + 1) * T2B],
                                start=(hc == 0), stop=(hc == 15)))(base, ic4, hc)
                            for hc in range(16)],
                            reads=(f"wgu{pb}", "y1f"), writes=(f"b{ic4}",))
                        b.G("tensor", [
                            (lambda base, ic4, hc: lambda e: nc.tensor.matmul(
                                bank(4 + ic4),
                                wgu[:, base + 8192 + hc * 512 + ic4 * 128: base + 8192 + hc * 512 + ic4 * 128 + 128],
                                y1f[:, hc * T2B:(hc + 1) * T2B],
                                start=(hc == 0), stop=(hc == 15)))(base, ic4, hc)
                            for hc in range(16)],
                            reads=(f"wgu{pb}", "y1f"), writes=(f"b{4 + ic4}",))
                        ic = ig * 4 + ic4
                        b.E("scalar", (lambda ic, ic4: lambda e: nc.scalar.activation(
                            gs_sb[:, :], bank(ic4), AF.Silu,
                            scale=sg_sb[:, ic:ic + 1]))(ic, ic4),
                            reads=(f"b{ic4}", "sg_sb"), writes=("gs_sb",))
                        b.E("vector", (lambda ic, ic4: lambda e: nc.vector.scalar_tensor_tensor(
                            fuse[:, ic * T2B:(ic + 1) * T2B], bank(4 + ic4),
                            su_sb[:, ic:ic + 1], gs_sb[:, :],
                            ALU.mult, ALU.mult))(ic, ic4),
                            reads=(f"b{4 + ic4}", "su_sb", "gs_sb"), writes=("fuse",))

                if ti + 1 < N2B:
                    emit_wgu_loads(ti + 1, 0)
                    emit_wgu_loads(ti + 1, 1)
                    loaded.add((ti + 1, 0))
                    loaded.add((ti + 1, 1))
                    emit_y1f_loads(ti + 1)

                for tp in range(2):
                    b.E("sync", (lambda t0, tp: lambda e: e.dma_start(
                        out=y1tok[:, :],
                        in_=y1_scr[t0 + tp * 256: t0 + tp * 256 + 128, :]))(t0, tp),
                        writes=("y1tok", "sdr"), sem="dS", amt=16)

                    for ip in range(32):
                        par = ip % 3
                        src2 = wd_i8[ip * 256:(ip + 1) * 256, :].rearrange(
                            "(a p) h -> p a h", p=128)
                        dst2 = wd_sb[:, par * 2 * D:(par + 1) * 2 * D].rearrange(
                            "p (a h) -> p a h", a=2)
                        b.E("gpsimd", (lambda dst2, src2: lambda e: e.dma_start(
                            out=dst2, in_=src2))(dst2, src2),
                            writes=(f"wd{par}",), sem="dG", amt=16)
                        fns = []
                        for a in range(2):
                            i = ip * 2 + a
                            for tc2 in range(2):
                                tg = tp * 2 + tc2
                                for ns in range(4):
                                    fns.append((lambda i, par, a, tc2, tg, ns: lambda e: nc.tensor.matmul(
                                        arena[:, (tc2 * 4 + ns) * 512:(tc2 * 4 + ns + 1) * 512],
                                        fuse[:, i * T2B + tg * 128: i * T2B + (tg + 1) * 128],
                                        wd_sb[:, par * 2 * D + a * D + ns * 512:
                                               par * 2 * D + a * D + (ns + 1) * 512],
                                        start=(i == 0), stop=(i == 63),
                                        skip_group_check=True))(i, par, a, tc2, tg, ns))
                        b.G("tensor", fns, reads=("fuse", f"wd{par}"), writes=BANKS)

                    for tc2 in range(2):
                        tg = tp * 2 + tc2
                        bs = ("b0", "b1", "b2", "b3") if tc2 == 0 else ("b4", "b5", "b6", "b7")
                        b.E("vector", (lambda tc2: lambda e: nc.vector.tensor_tensor(
                            y2n_sb[:, :], arena[:, tc2 * 2048: (tc2 + 1) * 2048],
                            sdb[:, :], ALU.mult))(tc2),
                            reads=bs + ("sdb",), writes=("y2buf",))
                        b.E("vector", lambda e: nc.vector.tensor_tensor(
                            y2n_sb[:, :], y2n_sb[:, :], y1tok[:, :],
                            ALU.add), reads=("y1tok", "y2buf"), writes=("y2buf",))
                        if tc2 == 0:
                            b.E("sync", (lambda t0, tp: lambda e: e.dma_start(
                                out=y1tok[:, :],
                                in_=y1_scr[t0 + tp * 256 + 128: t0 + tp * 256 + 256, :]))(t0, tp),
                                writes=("y1tok",), sem="dS", amt=16)
                        b.E("scalar", lambda e: nc.scalar.activation(
                            dum2[:, 0:D], y2n_sb[:, :], AF.Copy, accum_out=st2[:, 0:1]),
                            reads=("y2buf",), writes=("dum2", "s20"))
                        b.E("vector", lambda e: nc.vector.tensor_tensor(
                            dum2[:, D:2 * D], y2n_sb[:, :], y2n_sb[:, :], ALU.mult),
                            reads=("y2buf",), writes=("dum2",))
                        b.E("scalar", lambda e: nc.scalar.activation(
                            dum2[:, 0:D], dum2[:, D:2 * D], AF.Copy,
                            accum_out=st2[:, 1:2]),
                            reads=("dum2",), writes=("dum2", "s21"))
                        b.E("scalar", lambda e: nc.scalar.mul(
                            st2[:, 2:3], st2[:, 0:1], c_cm[:, 0:1]),
                            reads=("s20", "c_cm"), writes=("s22",), hard=True)
                        b.E("scalar", lambda e: nc.scalar.mul(
                            st2[:, 3:4], st2[:, 1:2], c_cm[:, 0:1]),
                            reads=("s21", "c_cm"), writes=("s23",), hard=True)
                        b.E("scalar", lambda e: nc.scalar.mul(
                            st2[:, 4:5], st2[:, 2:3], st2[:, 2:3]),
                            reads=("s22",), writes=("s24",), hard=True)
                        b.E("scalar", lambda e: nc.scalar.mul(
                            st2[:, 8:9], st2[:, 0:1], c_cm[:, 0:1]),
                            reads=("s20", "c_cm"), writes=("s28",), hard=True)
                        b.E("scalar", lambda e: nc.scalar.activation(
                            st2[:, 5:6], st2[:, 4:5], AF.Identity,
                            scale=c_cm[:, 4:5], bias=st2[:, 3:4]),
                            reads=("s23", "s24", "c_cm"), writes=("s25",), hard=True)
                        b.E("scalar", lambda e: nc.scalar.activation(
                            st2[:, 6:7], st2[:, 5:6], AF.Ln, bias=c_cm[:, 2:3]),
                            reads=("s25", "c_cm"), writes=("s26",), hard=True)
                        b.E("scalar", lambda e: nc.scalar.activation(
                            st2[:, 6:7], st2[:, 6:7], AF.Exp, scale=c_cm[:, 3:4]),
                            reads=("s26", "c_cm"), writes=("s26",), hard=True)
                        b.E("scalar", lambda e: nc.scalar.mul(
                            st2[:, 8:9], st2[:, 0:1], c_cm[:, 0:1]),
                            reads=("s20", "c_cm"), writes=("s28",), hard=True)
                        b.E("scalar", lambda e: nc.scalar.mul(
                            st2[:, 7:8], st2[:, 6:7], st2[:, 2:3]),
                            reads=("s22", "s26"), writes=("s27",), hard=True)
                        b.E("scalar", lambda e: nc.scalar.mul(
                            st2[:, 7:8], st2[:, 7:8], c_cm[:, 4:5]),
                            reads=("s27", "c_cm"), writes=("s27",), hard=True)
                        b.E("scalar", lambda e: nc.scalar.mul(
                            st2[:, 8:9], st2[:, 0:1], c_cm[:, 0:1]),
                            reads=("s20", "c_cm"), writes=("s28",), hard=True)
                        b.E("scalar", (lambda tc2: lambda e: nc.scalar.activation(
                            y1tok[:, :], y2n_sb[:, :], AF.Identity,
                            scale=st2[:, 6:7], bias=st2[:, 7:8]))(tc2),
                            reads=("y2buf", "s26", "s27"), writes=("y1tok", "y2n"), hard=True)
                        if debug_dump:
                            idx = ti * 4 + tp * 2 + tc2
                            b.E("sync", (lambda idx: lambda e: e.dma_start(
                                out=dbg["dbg_st2"].ap()[idx, :, :], in_=st2[:, 0:8]))(idx),
                                reads=("s20", "s21", "s22", "s23", "s24", "s25", "s26", "s27"),
                                sem="dS", amt=16)
                        b.E("scalar", (lambda t0, tg: lambda e: e.dma_start(
                            out=y_out[t0 + tg * 128: t0 + (tg + 1) * 128, :],
                            in_=y1tok[:, :]))(t0, tg),
                            reads=("y2n", "y1tok"), sem="dO", amt=16)

        b.w("sync", ("dO", b.cnt["dO"]))
        b.o("sync", lambda e: e.nop())
        b.w("scalar", ("dO", b.cnt["dO"]))
        b.o("scalar", lambda e: e.nop())

        with nc.Block() as block:
            @block.tensor
            def _(eng):
                b.replay("tensor", eng, semobjs)

            @block.vector
            def _(eng):
                b.replay("vector", eng, semobjs)

            @block.scalar
            def _(eng):
                b.replay("scalar", eng, semobjs)

            @block.sync
            def _(eng):
                b.replay("sync", eng, semobjs)

            @block.gpsimd
            def _(eng):
                b.replay("gpsimd", eng, semobjs)

    return nc


def _host_consts(core):
    p = np.arange(128)
    m = np.arange(BLK)
    mtidx = np.zeros((128, BLK), np.float32)
    d = m[None, :] - p[:, None]
    mtidx[:, :] = np.where(d >= 0, -d, NEG)
    mtidx = mtidx.astype(ml_dtypes.bfloat16)
    arrq = np.ascontiguousarray(
        np.broadcast_to(-(m[None, :] + 1.0), (128, BLK))).astype(ml_dtypes.bfloat16)
    arrkt = np.zeros((128, 2), np.float32)
    for half in (0, 1):
        arrkt[:, half] = -(255.0 - (half * 128 + p))
    arrcf = np.ascontiguousarray(
        np.broadcast_to(-256.0 * np.arange(NBLK)[None, :], (128, NBLK))).astype(np.float32)
    arrbd = np.full((128, 1), -256.0, np.float32)
    iot = np.zeros((128, NCORES), np.float32)
    for r in range(NCORES):
        iot[:, r] = (-2048.0 * (core - 1 - r)) if r < core else NEG
    idn = np.eye(128, dtype=ml_dtypes.bfloat16)
    onesbf = np.ones((128, 1), ml_dtypes.bfloat16)
    onesf32 = np.ones((1, 128), np.float32)
    cmisc = np.zeros((128, 6), np.float32)
    cmisc[:, 0] = 1.0 / D
    cmisc[:, 1] = EPS_RMS
    cmisc[:, 2] = EPS_LN
    cmisc[:, 3] = -0.5
    cmisc[:, 4] = -1.0
    return dict(mtidx=mtidx, arrq=arrq, arrkt=arrkt, arrcf=arrcf, arrbd=arrbd,
                iotacc=iot, idn=idn, onesbf=onesbf, onesf32=onesf32, cmisc=cmisc)


_NC_CACHE = {}


def _get_nc(debug_dump=False):
    if debug_dump not in _NC_CACHE:
        _NC_CACHE[debug_dump] = build(debug_dump)
    return _NC_CACHE[debug_dump]


def kernel(L0_attn_x, L0_attn_slope_rate, L0_attn_W_qkvu, L0_attn_W_out,
           L0_ffn_w_gate, L0_ffn_w_up, L0_ffn_w_down,
           L0_ffn_s_gate, L0_ffn_s_up, L0_ffn_s_down, _debug=False):
    x = np.ascontiguousarray(np.asarray(L0_attn_x, np.float32).reshape(SEQ, D))
    wq = np.ascontiguousarray(np.asarray(L0_attn_W_qkvu, np.float32))
    wo = np.ascontiguousarray(np.asarray(L0_attn_W_out, np.float32))
    wg = np.ascontiguousarray(np.asarray(L0_ffn_w_gate, np.int8))
    wu = np.ascontiguousarray(np.asarray(L0_ffn_w_up, np.int8))
    wd = np.ascontiguousarray(np.asarray(L0_ffn_w_down, np.int8))
    sg = np.ascontiguousarray(np.asarray(L0_ffn_s_gate, np.float32).reshape(64, 128).T)
    su = np.ascontiguousarray(np.asarray(L0_ffn_s_up, np.float32).reshape(64, 128).T)
    sd = np.ascontiguousarray(np.asarray(L0_ffn_s_down, np.float32).reshape(1, D))
    sl = np.ascontiguousarray(np.asarray(L0_attn_slope_rate, np.float32).reshape(1, H))

    nc = _get_nc(_debug)
    in_maps = []
    for c in range(NCORES):
        m = dict(x=np.ascontiguousarray(x[c * TC:(c + 1) * TC]),
                 wqkvu=wq, wout=wo, wg=wg, wu=wu, wd=wd,
                 sgate=sg, sup=su, sdown=sd, slope=sl)
        m.update(_host_consts(c))
        in_maps.append(m)
    res = run_bass_kernel_spmd(nc, in_maps, core_ids=list(range(NCORES)))
    if _debug:
        return res
    out = np.concatenate([np.asarray(res.results[c]["y"]) for c in range(NCORES)], axis=0)
    return out.reshape(1, SEQ, D)
